# version 09 (final)
# baseline (speedup 1.0000x reference)
import numpy as np
from contextlib import ExitStack

import concourse.bass as bass
import concourse.mybir as mybir
from concourse.bass_utils import run_bass_kernel_spmd

F32 = mybir.dt.float32
BF16 = mybir.dt.bfloat16
AF = mybir.ActivationFunctionType
ALU = mybir.AluOpType
AX = mybir.AxisListType

TOK = 2048
NT = TOK // 128
NS = TOK // 512
C = 1024
FC = C // 128
EPS_LN = 1e-5

ENGS = ("sy", "gp", "ve", "se", "pe")



class Buf:
    __slots__ = ("h", "name", "last_w", "readers", "pend_w")

    def __init__(self, h, name):
        self.h = h
        self.name = name
        self.last_w = None
        self.readers = []
        self.pend_w = None

    def __getitem__(self, idx):
        return self.h[idx]

    def ap(self):
        return self.h.ap() if hasattr(self.h, "ap") else self.h


class Prog:
    def __init__(self, name="prog"):
        self.nc = bass.Bass("TRN2", target_bir_lowering=False, debug=False)
        self.name = name
        self.rec = {e: [] for e in ENGS}
        self.cnt = {e: 0 for e in ENGS}
        self.cnt["cc"] = 0
        for q in ("sy", "gp"):
            for i in range(4):
                self.cnt[f"{q}d{i}"] = 0
        self.lane_idx = {"sy": 0, "gp": 0}
        self.waited = {}
        self.ctx = ExitStack()
        self.sems = {}
        self.bufs = []
        self.out_dma = []

    def sbuf(self, name, shape, dt):
        h = self.ctx.enter_context(self.nc.sbuf_tensor(name, list(shape), dt))
        b = Buf(h, name)
        self.bufs.append(b)
        return b

    def psum(self, name, shape, dt=F32):
        h = self.ctx.enter_context(self.nc.psum_tensor(name, list(shape), dt))
        b = Buf(h, name)
        self.bufs.append(b)
        return b

    def dram(self, name, shape, dt, kind=None):
        if kind is None:
            t = self.nc.dram_tensor(name, list(shape), dt)
        else:
            t = self.nc.dram_tensor(name, list(shape), dt, kind=kind)
        b = Buf(t, name)
        self.bufs.append(b)
        return b

    def scope(self):
        return self.ctx.enter_context(ExitStack())

    def _wait(self, eng, src, val, waits):
        if val is None:
            return
        key = (eng, src)
        if self.waited.get(key, -1) >= val:
            return
        self.waited[key] = val
        waits.append((src, val))

    DEBUG_NEXT = [0]

    def op(self, eng, fn, reads=(), writes=(), sig=True, dma=False, sig_sem=None,
           is_output=False, extra_waits=()):
        waits = list(extra_waits)
        dbg = Prog.DEBUG_NEXT[0] > 0
        if dbg:
            Prog.DEBUG_NEXT[0] -= 1
        for b in reads:
            if b.pend_w is not None and b.pend_w != eng:
                raise RuntimeError(f"unsignaled write on {b.name} (from {b.pend_w}, read by {eng})")
            if b.last_w and b.last_w[1] is not None:
                self._wait(eng, b.last_w[0], b.last_w[1], waits)
        for b in writes:
            if b.pend_w is not None and b.pend_w != eng:
                raise RuntimeError(f"unsignaled WAW on {b.name}")
            if b.last_w and b.last_w[1] is not None:
                self._wait(eng, b.last_w[0], b.last_w[1], waits)
            for (re, rv) in b.readers:
                if rv is None:
                    if re != eng:
                        raise RuntimeError(f"unsignaled read of {b.name} (by {re}, overwritten by {eng})")
                else:
                    self._wait(eng, re, rv, waits)
        if dbg:
            print("DBG op", eng, "reads", [b.name for b in reads],
                  "lastw", [(b.name, b.last_w) for b in reads], "waits", waits)
        sem_name = sig_sem or eng
        if sig:
            self.cnt[sem_name] += 16 if dma else 1
            val = self.cnt[sem_name]
        else:
            val = None
        for b in writes:
            b.last_w = (sem_name, val) if sig else (eng, None)
            b.pend_w = None if sig else eng
            b.readers = []
        for b in reads:
            b.readers.append((sem_name if sig else eng, val))
        if is_output and sig:
            self.out_dma.append((sem_name, val))

        def closure(engobj, waits=waits, fn=fn, sig=sig, dma=dma,
                    sem_name=sem_name):
            for (s, v) in waits:
                engobj.wait_ge(self.sems[s], v)
            ins = fn(engobj)
            if sig:
                ins.then_inc(self.sems[sem_name], 16 if dma else 1)
            return ins

        self.rec[eng].append(closure)

    def barrier(self, engs=ENGS):
        arr = {e: self.cnt[e] for e in engs}
        lanes = [(f"{q}d{i}", self.cnt[f"{q}d{i}"]) for q in ("sy", "gp")
                 for i in range(4) if self.cnt.get(f"{q}d{i}", 0) > 0]
        for e in engs:
            waits = []
            for s in engs:
                if s != e and arr[s] > 0:
                    self._wait(e, s, arr[s], waits)
            for (sn, v) in lanes:
                self._wait(e, sn, v, waits)

            def wait_fn(engobj, waits=waits):
                for (sn, v) in waits:
                    engobj.wait_ge(self.sems[sn], v)
            self.rec[e].append(wait_fn)
        for b in self.bufs:
            if b.last_w is not None and b.last_w[1] is not None:
                b.last_w = None
            b.readers = [r for r in b.readers if r[1] is None]

    def dma(self, queue, out, in_, reads=(), writes=(), is_output=False, **kw):
        li = self.lane_idx[queue] % 4
        self.lane_idx[queue] += 1
        lane = f"{queue}d{li}"
        ew = []
        if self.cnt[lane] > 0:
            key = (queue, lane)
            if self.waited.get(key, -1) < self.cnt[lane]:
                self.waited[key] = self.cnt[lane]
                ew.append((lane, self.cnt[lane]))

        def fn(engobj):
            return engobj.dma_start(out=out, in_=in_, **kw)
        self.op(queue, fn, reads=reads, writes=writes, sig=True, dma=True,
                is_output=is_output, sig_sem=lane, extra_waits=ew)

    def build(self):
        for q in ("sy", "gp"):
            waits = [(f"{q}d{i}", self.cnt[f"{q}d{i}"]) for i in range(4)
                     if self.cnt[f"{q}d{i}"] > 0]
            if waits:
                def wfn(engobj, waits=waits):
                    for (sn, v) in waits:
                        engobj.wait_ge(self.sems[sn], v)
                self.rec[q].append(wfn)
        with self.ctx:
            for e in ENGS:
                self.sems[e] = self.ctx.enter_context(self.nc.semaphore(f"sem_{e}"))
            self.sems["cc"] = self.ctx.enter_context(self.nc.semaphore("sem_cc"))
            for q in ("sy", "gp"):
                for i in range(4):
                    self.sems[f"{q}d{i}"] = self.ctx.enter_context(
                        self.nc.semaphore(f"sem_{q}d{i}"))
            with self.nc.Block() as block:
                sec = dict(sy=block.sync, gp=block.gpsimd, ve=block.vector,
                           se=block.scalar, pe=block.tensor)
                for e in ENGS:
                    items = self.rec[e]
                    if not items:
                        continue

                    def make_body(items=items):
                        def body(engobj):
                            for it in items:
                                it(engobj)
                        return body
                    sec[e](make_body())
        return self.nc



class Core:
    def __init__(self, p: Prog):
        self.p = p
        self.ident = p.sbuf("c_ident", [128, 128], F32)
        self.identb = p.sbuf("c_identb", [128, 128], BF16)
        self.onesrow = p.sbuf("c_onesrow", [1, 512], BF16)
        self.onescol = p.sbuf("c_onescol", [128, 1], BF16)
        self.onesf = p.sbuf("c_onesf", [1, 128], F32)
        self.onescolF = p.sbuf("c_onescolF", [128, 1], F32)
        self.stats = p.sbuf("stats", [128, 64], F32)
        self.rows = p.sbuf("rows", [2, TOK], BF16)
        self.tmp32 = p.sbuf("tmp32", [32, 128], F32)
        self.diag = p.sbuf("diag", [128, 128], F32)
        self.xld = [p.sbuf("xld0", [128, 1024], F32)] * 2
        self.wstage = [p.sbuf("wstage0", [128, 2048], F32)] * 2
        self.dump = None
        self.wbf = [p.sbuf(f"wbf{i}", [128, 4096], BF16) for i in range(2)]
        self.dump = self.wstage[0]
        self.rowf = self.wstage[0]
        self.colr = [p.sbuf(f"colr{i}", [1, 1024], BF16) for i in range(2)]
        self.colb = [p.sbuf("colb0", [1, 1024], BF16)] * 2
        self.rb_dram = p.dram("rowbounce", [32, 128], F32)
        self.epscol = p.sbuf("epscol", [128, 1], F32)
        self.pb = [p.psum(f"pb{i}", [128, 512], F32) for i in range(6)]
        self.ptr = p.psum("ptr", [128, 512], F32)
        self.pmisc = p.psum("pmisc", [128, 512], F32)
        self._rr = 0
        p.op("ve", lambda e: e.memset(self.onesrow[0:1, :], 1.0), writes=[self.onesrow])
        p.op("ve", lambda e: e.memset(self.onescol[:, :], 1.0), writes=[self.onescol])
        p.op("ve", lambda e: e.memset(self.epscol[:, :], EPS_LN), writes=[self.epscol])
        p.op("ve", lambda e: e.memset(self.onesf[0:1, :], 1.0), writes=[self.onesf])
        p.op("ve", lambda e: e.memset(self.onescolF[:, :], 1.0), writes=[self.onescolF])

    def bank(self):
        b = self.pb[self._rr % 6]
        self._rr += 1
        return b

    def bank2(self):
        i = self._rr % 6
        if i == 5:
            self._rr += 1
            i = 0
        self._rr += 2
        return self.pb[i], self.pb[i + 1] if i + 1 < 6 else self.pb[0]


def emit_stats(core, x_src, src_rows=TOK):
    p = core.p
    st = core.stats
    p.op("ve", lambda e: e.memset(st[:, 0:32], 0.0), writes=[st])
    for i in range(NT):
        buf = core.xld[i % 2]
        p.dma("sy", buf[:, :], x_src.ap()[i * 128:(i + 1) * 128, :],
              reads=[x_src], writes=[buf])
        p.op("se", lambda e, i=i, buf=buf: e.activation(
            out=core.dump[:, 0:1024], in_=buf[:, :], func=AF.Copy,
            accum_out=st[:, i:i + 1]), reads=[buf], writes=[core.dump, st])
        p.op("se", lambda e, i=i, buf=buf: e.activation(
            out=core.dump[:, 0:1024], in_=buf[:, :], func=AF.Square,
            accum_out=st[:, 16 + i:17 + i]), reads=[buf], writes=[core.dump, st])
    finish_stats(core)


def finish_stats(core):
    p = core.p
    st = core.stats
    n = float(C)
    p.op("ve", lambda e: e.tensor_scalar(out=st[:, 32:48], in0=st[:, 0:16],
         scalar1=1.0 / n, scalar2=None, op0=ALU.mult), reads=[st], writes=[st])
    p.op("ve", lambda e: e.tensor_scalar(out=st[:, 16:32], in0=st[:, 16:32],
         scalar1=1.0 / n, scalar2=None, op0=ALU.mult), reads=[st], writes=[st])
    p.op("ve", lambda e: e.tensor_tensor(out=st[:, 0:16], in0=st[:, 32:48],
         in1=st[:, 32:48], op=ALU.mult), reads=[st], writes=[st])
    p.op("ve", lambda e: e.tensor_tensor(out=st[:, 16:32], in0=st[:, 16:32],
         in1=st[:, 0:16], op=ALU.subtract), reads=[st], writes=[st])
    p.op("se", lambda e: e.activation(out=st[:, 0:16], in_=st[:, 16:32],
         func=AF.Ln, bias=core.epscol[:, 0:1]), reads=[st, core.epscol], writes=[st])
    p.op("se", lambda e: e.activation(out=st[:, 48:64], in_=st[:, 0:16],
         func=AF.Exp, scale=-0.5), reads=[st], writes=[st])
    p.op("ve", lambda e: e.scalar_tensor_tensor(out=st[:, 0:16], in0=st[:, 32:48],
         scalar=-1.0, in1=st[:, 48:64], op0=ALU.mult, op1=ALU.mult),
         reads=[st], writes=[st])
    emit_rows(core, st, 0, 48)


def emit_rows(core, st, negmr_col, rstd_col):
    p = core.p
    pk = core.dump
    p.op("ve", lambda e: e.tensor_copy(out=pk[:, 0:16], in_=st[:, negmr_col:negmr_col + 16]),
         reads=[st], writes=[pk])
    p.op("pe", lambda e: e.transpose(core.ptr[0:16, 0:128], pk[:, 0:16], core.ident[:, :]),
         reads=[pk, core.ident], writes=[core.ptr])
    p.op("ve", lambda e: e.tensor_copy(out=core.tmp32[0:16, :], in_=core.ptr[0:16, 0:128]),
         reads=[core.ptr], writes=[core.tmp32])
    p.dma("gp", core.rb_dram.ap()[0:16, :], core.tmp32.h[0:16, :],
          reads=[core.tmp32], writes=[core.rb_dram])
    p.dma("gp", core.rowf[0:1, :],
          core.rb_dram.ap()[0:16, :].rearrange("(a t) p -> a (t p)", a=1),
          reads=[core.rb_dram], writes=[core.rowf])
    p.op("ve", lambda e: e.memset(core.rows[0:2, :], 1.0), writes=[core.rows])
    p.op("ve", lambda e: e.tensor_copy(out=core.rows[0:1, :], in_=core.rowf[0:1, :]),
         reads=[core.rowf], writes=[core.rows])


def emit_xt(core, x_src, dst, rstd_col=48, gamma=None, dst_col0=0, dst_stride=TOK):
    p = core.p
    st = core.stats
    for i in range(NT):
        buf = core.xld[i % 2]
        p.dma("sy", buf[:, :], x_src.ap()[i * 128:(i + 1) * 128, :],
              reads=[x_src], writes=[buf])
        p.op("ve", lambda e, i=i: e.tensor_scalar(
            out=core.diag[:, :], in0=core.ident[:, :],
            scalar1=st[:, rstd_col + i:rstd_col + i + 1],
            scalar2=None, op0=ALU.mult), reads=[core.ident, st], writes=[core.diag])
        for h in range(2):
            def mmfn(e, i=i, h=h, buf=buf):
                r = None
                for k in range(4):
                    fc = h * 4 + k
                    r = e.matmul(core.ptr[:, k * 128:(k + 1) * 128],
                                 buf[:, fc * 128:(fc + 1) * 128],
                                 core.diag[:, :], start=True, stop=True)
                return r
            p.op("pe", mmfn, reads=[buf, core.diag], writes=[core.ptr])
            for k in range(4):
                fc = h * 4 + k
                c0 = fc * dst_stride + dst_col0 + i * 128
                if gamma is None:
                    p.op("ve", lambda e, c0=c0, k=k: e.tensor_copy(
                        out=dst[:, c0:c0 + 128],
                        in_=core.ptr[:, k * 128:(k + 1) * 128]),
                        reads=[core.ptr], writes=[dst])
                else:
                    gbuf, gc0 = gamma
                    p.op("ve", lambda e, c0=c0, fc=fc, k=k, gbuf=gbuf, gc0=gc0: e.tensor_scalar(
                        out=dst[:, c0:c0 + 128],
                        in0=core.ptr[:, k * 128:(k + 1) * 128],
                        scalar1=gbuf[:, gc0 + fc:gc0 + fc + 1], scalar2=None, op0=ALU.mult),
                        reads=[core.ptr, gbuf], writes=[dst])



def emit_load_ident(core, cident_dram):
    core.p.dma("sy", core.ident[:, :], cident_dram.ap()[:, :],
               reads=[cident_dram], writes=[core.ident])
    core.p.op("ve", lambda e: e.tensor_copy(out=core.identb[:, :], in_=core.ident[:, :]),
              reads=[core.ident], writes=[core.identb])


def stream_wcols(core, w_dram, rows, col0, col1, which, colsum=True, bias=None,
                 wb_off=0, cr_off=0, gamma_col=None, row0=0,
                 colsum_g=None, bias_g=None):
    p = core.p
    cols = col1 - col0
    kc = rows // 128
    assert wb_off + kc * cols <= 4096
    stage = core.wstage[which]
    wb = core.wbf[which]
    cr = core.colr[which]
    cb = core.colb[which]
    stage_cols = 2048
    rows_per = 128
    while (rows_per * 2 // 128) * cols <= stage_cols and rows_per * 2 <= 512:
        rows_per *= 2
    done = 0
    while done < rows:
        r1 = min(rows, done + rows_per)
        blk = ((r1 - done) // 128) * cols
        nk = (r1 - done) // 128
        p.dma("sy", stage.h[:, 0:blk].rearrange("p (k c) -> p k c", k=nk),
              w_dram.ap()[row0 + done:row0 + r1, col0:col1].rearrange("(k p) c -> p k c", p=128),
              reads=[w_dram], writes=[stage])
        if gamma_col is None:
            p.op("ve", lambda e, done=done, blk=blk: e.tensor_copy(
                out=wb[:, wb_off + (done // 128) * cols:wb_off + (done // 128) * cols + blk],
                in_=stage[:, 0:blk]), reads=[stage], writes=[wb])
        else:
            gbuf, gc0 = gamma_col
            for k in range(done // 128, r1 // 128):
                p.op("ve", lambda e, k=k, done=done, gbuf=gbuf, gc0=gc0: e.tensor_scalar(
                    out=wb[:, wb_off + k * cols:wb_off + (k + 1) * cols],
                    in0=stage[:, (k - done // 128) * cols:(k - done // 128 + 1) * cols],
                    scalar1=gbuf[:, gc0 + k:gc0 + k + 1], scalar2=None, op0=ALU.mult),
                    reads=[stage, gbuf], writes=[wb])
        done = r1
    if colsum or colsum_g is not None:
        pm = core.pmisc
        if colsum_g is None:
            lcol, lc0 = core.onescol, None
        else:
            lcol, lc0 = colsum_g

        def mmfn(e, lcol=lcol, lc0=lc0):
            r = None
            for k in range(kc):
                l = lcol[:, 0:1] if lc0 is None else lcol[:, lc0 + k:lc0 + k + 1]
                r = e.matmul(pm[0:1, 0:cols], l,
                             wb[:, wb_off + k * cols:wb_off + (k + 1) * cols],
                             start=(k == 0), stop=(k == kc - 1))
            return r
        p.op("pe", mmfn, reads=[wb, lcol], writes=[pm])
        p.op("ve", lambda e: e.tensor_copy(out=cr[0:1, cr_off:cr_off + cols],
             in_=pm[0:1, 0:cols]), reads=[pm], writes=[cr])
    if bias_g is not None:
        pm = core.pmisc
        gcol, gc0 = bias_g

        def mmfn2(e, gcol=gcol, gc0=gc0):
            r = None
            for k in range(kc):
                r = e.matmul(pm[0:1, 0:cols], gcol[:, gc0 + k:gc0 + k + 1],
                             wb[:, wb_off + k * cols:wb_off + (k + 1) * cols],
                             start=(k == 0), stop=(k == kc - 1))
            return r
        p.op("pe", mmfn2, reads=[wb, gcol], writes=[pm])
        p.op("ve", lambda e: e.tensor_copy(out=cb[0:1, cr_off:cr_off + cols],
             in_=pm[0:1, 0:cols]), reads=[pm], writes=[cb])
    if bias is not None:
        p.dma("sy", core.rowf[0:1, 0:cols],
              bias.ap()[col0:col1].rearrange("(a c) -> a c", a=1),
              reads=[bias], writes=[core.rowf])
        p.op("ve", lambda e: e.tensor_copy(out=cb[0:1, cr_off:cr_off + cols],
             in_=core.rowf[0:1, 0:cols]), reads=[core.rowf], writes=[cb])


def mm_chain(p, psum_view, pairs, reads, writes, extras=(), start=True, stop=True):
    allp = list(pairs) + list(extras)

    def fn(e):
        r = None
        n = len(allp)
        for i, (l, rr) in enumerate(allp):
            r = e.matmul(psum_view, l, rr, start=(start and i == 0),
                         stop=(stop and i == n - 1))
        return r
    p.op("pe", fn, reads=reads, writes=writes)


def load_fmcol(core, buf, vec_dram, n, col0=0):
    core.p.dma("gp", buf[:, col0:col0 + n // 128],
               vec_dram.ap().rearrange("(c p) -> p c", p=128),
               reads=[vec_dram], writes=[buf], allow_slow_non_contiguous=True)


def emit_residual_tiles(core, x_src, x_dst, coef, blocks_of, tile_range=None):
    raise NotImplementedError


def emit_l2_ffn(core, W, x_src, x_dst, sc1, sc2):
    import os
    PH = os.environ.get("L2PH", "ABCD")
    """x_dst = x_src + 0.5 * L2ffn(LN(x_src))."""
    p = core.p
    xt = p.sbuf("l2xt", [128, FC * TOK], BF16)
    hfull = p.sbuf("l2h", [128, 32 * TOK], BF16)
    scol = p.sbuf("l2scol", [128, 128], F32)
    w3scol = p.sbuf("l2w3s", [128, 8], F32)
    scrA = p.sbuf("l2scrA", [128, 1024], F32)
    uscr = scrA
    xnew = scrA
    hstg = p.sbuf("l2hstg", [128, 512], BF16)

    for z in range(8):
        p.op("ve", lambda e, z=z: e.memset(hfull[:, z * 8192:(z + 1) * 8192], 0.0),
             writes=[hfull])
    dbg = os.environ.get("L2DBG") == "1"
    if dbg:
        d_rows = p.dram("d_rows", [2, TOK], BF16, "ExternalOutput")
        d_h = p.dram("d_h", [128, TOK], BF16, "ExternalOutput")
        d_sc1 = p.dram("d_sc1", [128, TOK], BF16, "ExternalOutput")
        d_xt = p.dram("d_xt", [128, TOK], BF16, "ExternalOutput")
    emit_stats(core, x_src)
    emit_xt(core, x_src, xt)
    if dbg:
        p.dma("gp", d_rows.ap()[:, :], core.rows[0:2, :], reads=[core.rows],
              writes=[d_rows], is_output=True)
        p.dma("gp", d_xt.ap()[:, :], xt[:, 0:TOK], reads=[xt], writes=[d_xt],
              is_output=True)
    for j, nm in enumerate(("s1", "s2", "h1s", "h2s")):
        load_fmcol(core, scol, W[nm], 4096, j * 32)
    load_fmcol(core, w3scol, W["w3s"], 1024)

    for oc in range(32 if "A" in PH else 0):
        which = oc % 2
        stream_wcols(core, W["w1T"], 1024, oc * 128, (oc + 1) * 128, which,
                     colsum=True, cr_off=0, wb_off=0)
        stream_wcols(core, W["w2T"], 1024, oc * 128, (oc + 1) * 128, which,
                     colsum=True, cr_off=128, wb_off=1024)
        wb = core.wbf[which]
        cr = core.colr[which]
        for s in range(NS):
            pu = core.bank()
            pv = core.bank()
            sl = slice(s * 512, (s + 1) * 512)
            mm_chain(p, pu[:, 0:512],
                     [(wb[:, k * 128:(k + 1) * 128],
                       xt[:, k * TOK + s * 512:k * TOK + (s + 1) * 512]) for k in range(FC)],
                     reads=[wb, cr, core.rows, xt], writes=[pu],
                     extras=[(cr[0:1, 0:128], core.rows[0:1, sl])])
            mm_chain(p, pv[:, 0:512],
                     [(wb[:, 1024 + k * 128:1024 + (k + 1) * 128],
                       xt[:, k * TOK + s * 512:k * TOK + (s + 1) * 512]) for k in range(FC)],
                     reads=[wb, cr, core.rows, xt], writes=[pv],
                     extras=[(cr[0:1, 128:256], core.rows[0:1, sl])])
            p.op("se", lambda e, oc=oc, pu=pu: e.activation(
                out=uscr[:, 0:512], in_=pu[:, 0:512], func=AF.Silu,
                scale=scol[:, oc:oc + 1]), reads=[pu, scol], writes=[uscr])
            p.op("ve", lambda e, oc=oc, sl=sl, pv=pv: e.scalar_tensor_tensor(
                out=hfull[:, oc * TOK + s * 512:oc * TOK + (s + 1) * 512], in0=pv[:, 0:512],
                scalar=scol[:, 32 + oc:33 + oc], in1=uscr[:, 0:512],
                op0=ALU.mult, op1=ALU.mult), reads=[pv, scol, uscr],
                writes=[hfull])

    Prog.DEBUG_NEXT[0] = 0
    if dbg:
        p.dma("gp", d_h.ap()[:, :], hfull[:, 0:TOK], reads=[hfull], writes=[d_h],
              is_output=True)
    phases_bc = []
    if "B" in PH:
        phases_bc.append(("h1T", 64, sc1))
    if "C" in PH:
        phases_bc.append(("h2T", 96, sc2))
    for phase, (wname, sbase, dram) in enumerate(phases_bc):
        for oc in range(32):
            which = oc % 2
            stream_wcols(core, W[wname], 4096, oc * 128, (oc + 1) * 128, which,
                         colsum=False)
            wb = core.wbf[which]
            for s in range(NS):
                pp = core.bank()
                sl = slice(s * 512, (s + 1) * 512)
                if phase == 0 and oc == 0 and s == 0:
                    Prog.DEBUG_NEXT[0] = 1
                mm_chain(p, pp[:, 0:512],
                         [(wb[:, k * 128:(k + 1) * 128],
                           hfull[:, k * TOK + s * 512:k * TOK + (s + 1) * 512]) for k in range(32)],
                         reads=[wb, hfull], writes=[pp])
                gfun = AF.Copy if os.environ.get("L2NOGELU") == "1" else AF.Gelu
                p.op("se", lambda e, oc=oc, pp=pp, sbase=sbase, gfun=gfun: e.activation(
                    out=hstg[:, :], in_=pp[:, 0:512], func=gfun,
                    scale=scol[:, sbase + oc:sbase + oc + 1]),
                    reads=[pp, scol], writes=[hstg])
                p.dma("gp", dram.ap()[oc * 128:(oc + 1) * 128, s * 512:(s + 1) * 512],
                      hstg[:, :], reads=[hstg], writes=[dram])
        if os.environ.get("L2NORELOAD") != "1":
            if os.environ.get("L2NOBAR") != "1":
                p.barrier(engs=tuple(os.environ.get("L2BARENGS", "sy,gp,ve,se,pe").split(",")))
            for k in range(32):
                p.dma("sy", hfull[:, k * TOK:(k + 1) * TOK],
                      dram.ap()[k * 128:(k + 1) * 128, :],
                      reads=[dram], writes=[hfull])

    if dbg:
        p.dma("gp", d_sc1.ap()[:, :], hfull[:, 0:TOK], reads=[hfull],
              writes=[d_sc1], is_output=True)
    ofm = xt
    for oc in range(FC if "D" in PH else 0):
        which = oc % 2
        stream_wcols(core, W["w3T"], 4096, oc * 128, (oc + 1) * 128, which,
                     colsum=False)
        wb = core.wbf[which]
        for s in range(NS):
            pp = core.bank()
            sl = slice(s * 512, (s + 1) * 512)
            mm_chain(p, pp[:, 0:512],
                     [(wb[:, k * 128:(k + 1) * 128],
                       hfull[:, k * TOK + s * 512:k * TOK + (s + 1) * 512]) for k in range(32)],
                     reads=[wb, hfull], writes=[pp])
            p.op("ve", lambda e, oc=oc, s=s, pp=pp: e.tensor_scalar(
                out=ofm[:, oc * TOK + s * 512:oc * TOK + (s + 1) * 512], in0=pp[:, 0:512],
                scalar1=w3scol[:, oc:oc + 1], scalar2=None, op0=ALU.mult),
                reads=[pp, w3scol], writes=[ofm])

    emit_epilogue_fm(core, ofm, x_src, x_dst, 0.5, xnew)


def emit_epilogue_fm(core, ofm, x_src, x_dst, coef, xnew, extra_fn=None):
    p = core.p
    ptrb_ap = core.ptr.h[:, :].bitcast(BF16)
    for i in range(NT):
        buf = core.xld[i % 2]
        p.dma("sy", buf[:, :], x_src.ap()[i * 128:(i + 1) * 128, :],
              reads=[x_src], writes=[buf])
        for h in range(2):
            def trf(e, i=i, h=h):
                r = None
                for k in range(4):
                    fc = h * 4 + k
                    r = e.transpose(ptrb_ap[:, k * 128:(k + 1) * 128],
                                    ofm[:, fc * TOK + i * 128:fc * TOK + (i + 1) * 128],
                                    core.identb[:, :])
                return r
            p.op("pe", trf, reads=[ofm, core.identb],
                 writes=[core.ptr])
            for k in range(4):
                fc = h * 4 + k
                p.op("ve", lambda e, i=i, fc=fc, k=k, buf=buf: e.scalar_tensor_tensor(
                    out=xnew[:, fc * 128:(fc + 1) * 128],
                    in0=ptrb_ap[:, k * 128:(k + 1) * 128], scalar=coef,
                    in1=buf[:, fc * 128:(fc + 1) * 128],
                    op0=ALU.mult, op1=ALU.add), reads=[core.ptr, buf], writes=[xnew])
        p.dma("gp", x_dst.ap()[i * 128:(i + 1) * 128, :], xnew[:, :],
              reads=[xnew], writes=[x_dst], is_output=True)
        if extra_fn is not None:
            extra_fn(i)



L2FFN_W = dict(w1T=[1024, 4096], w2T=[1024, 4096], h1T=[4096, 4096],
               h2T=[4096, 4096], w3T=[4096, 1024], s1=[4096], s2=[4096],
               h1s=[4096], h2s=[4096], w3s=[1024])


def build_l2_ffn_program():
    p = Prog("l2ffn")
    x_in = p.dram("x_in", [TOK, C], F32, "ExternalInput")
    x_out = p.dram("x_out", [TOK, C], F32, "ExternalOutput")
    cident = p.dram("cident", [128, 128], F32, "ExternalInput")
    W = {k: p.dram(k, v, F32, "ExternalInput") for k, v in L2FFN_W.items()}
    sc1 = p.dram("sc1", [4096, TOK], BF16)
    sc2 = p.dram("sc2", [4096, TOK], BF16)
    core = Core(p)
    emit_load_ident(core, cident)
    emit_l2_ffn(core, W, x_in, x_out, sc1, sc2)
    return p.build()


def host_l2_ffn_weights(inp):
    T = lambda a: np.ascontiguousarray(np.asarray(a).T.astype(np.float32))
    return dict(
        w1T=T(inp["L2_ffn_w1_w"]), w2T=T(inp["L2_ffn_w2_w"]),
        h1T=T(inp["L2_ffn_h1_w"]), h2T=T(inp["L2_ffn_h2_w"]),
        w3T=T(inp["L2_ffn_w3_w"]),
        s1=np.asarray(inp["L2_ffn_w1_s"], np.float32),
        s2=np.asarray(inp["L2_ffn_w2_s"], np.float32),
        h1s=np.asarray(inp["L2_ffn_h1_s"], np.float32),
        h2s=np.asarray(inp["L2_ffn_h2_s"], np.float32),
        w3s=np.asarray(inp["L2_ffn_w3_s"], np.float32),
    )


IDENT = np.eye(128, dtype=np.float32)



S1 = TOK + 8


def emit_l1_ffn(core, W, x_src, x_dst, xprev):
    p = core.p
    xth = p.sbuf("l1xth", [128, FC * S1], BF16)
    rowsh = p.sbuf("l1rowsh", [1, S1], BF16)
    rsig = p.sbuf("l1rsig", [128, NT * 1024], BF16)
    ka = p.sbuf("l1ka", [128, FC * TOK], BF16)
    mucol = p.sbuf("l1mu", [128, 32], F32)
    hrow = p.sbuf("l1hrow", [1, 1024], F32)
    hst = p.sbuf("l1hst", [1, 8], F32)
    scr = p.sbuf("l1scr", [128, 1024], F32)

    emit_stats(core, x_src)
    emit_xt(core, x_src, xth, dst_col0=8, dst_stride=S1)
    p.op("ve", lambda e: e.memset(rowsh[0:1, :], 0.0), writes=[rowsh])
    p.op("ve", lambda e: e.tensor_copy(out=rowsh[0:1, 8:8 + TOK], in_=core.rows[0:1, :]),
         reads=[core.rows], writes=[rowsh])
    load_fmcol(core, mucol, W["mu_r"], 1024, 0)
    load_fmcol(core, mucol, W["mu_k"], 1024, 16)
    p.op("ve", lambda e: e.tensor_scalar(out=mucol[:, 8:16], in0=mucol[:, 0:8],
         scalar1=-1.0, scalar2=1.0, op0=ALU.mult, op1=ALU.add), reads=[mucol], writes=[mucol])
    p.op("ve", lambda e: e.tensor_scalar(out=mucol[:, 24:32], in0=mucol[:, 16:24],
         scalar1=-1.0, scalar2=1.0, op0=ALU.mult, op1=ALU.add), reads=[mucol], writes=[mucol])

    p.dma("sy", hrow[0:1, :], xprev.ap()[0:1, :], reads=[xprev], writes=[hrow])
    p.op("se", lambda e: e.activation(out=scr[0:1, :], in_=hrow[0:1, :], func=AF.Copy,
         accum_out=hst[0:1, 0:1]), reads=[hrow], writes=[scr, hst])
    p.op("se", lambda e: e.activation(out=scr[0:1, :], in_=hrow[0:1, :], func=AF.Square,
         accum_out=hst[0:1, 1:2]), reads=[hrow], writes=[scr, hst])
    p.op("ve", lambda e: e.tensor_scalar(out=hst[0:1, 2:4], in0=hst[0:1, 0:2],
         scalar1=1.0 / C, scalar2=None, op0=ALU.mult), reads=[hst], writes=[hst])
    p.op("ve", lambda e: e.tensor_tensor(out=hst[0:1, 4:5], in0=hst[0:1, 2:3],
         in1=hst[0:1, 2:3], op=ALU.mult), reads=[hst], writes=[hst])
    p.op("ve", lambda e: e.tensor_tensor(out=hst[0:1, 4:5], in0=hst[0:1, 3:4],
         in1=hst[0:1, 4:5], op=ALU.subtract), reads=[hst], writes=[hst])
    p.op("se", lambda e: e.activation(out=hst[0:1, 5:6], in_=hst[0:1, 4:5],
         func=AF.Ln, bias=core.epscol[0:1, 0:1]), reads=[hst, core.epscol], writes=[hst])
    p.op("se", lambda e: e.activation(out=hst[0:1, 6:7], in_=hst[0:1, 5:6],
         func=AF.Exp, scale=-0.5), reads=[hst], writes=[hst])
    p.op("ve", lambda e: e.scalar_tensor_tensor(out=hst[0:1, 7:8], in0=hst[0:1, 2:3],
         scalar=-1.0, in1=hst[0:1, 6:7], op0=ALU.mult, op1=ALU.mult),
         reads=[hst], writes=[hst])
    p.op("ve", lambda e: e.tensor_copy(out=rowsh[0:1, 7:8], in_=hst[0:1, 7:8]),
         reads=[hst], writes=[rowsh])
    xpb = p.view_bf = p.sbuf("l1xpb", [1, 1024], BF16)
    p.op("ve", lambda e: e.tensor_scalar(out=xpb[0:1, :], in0=hrow[0:1, :],
         scalar1=hst[0:1, 6:7], scalar2=None, op0=ALU.mult), reads=[hrow, hst], writes=[xpb])
    for fc in range(FC):
        p.op("pe", lambda e, fc=fc: e.matmul(
            core.ptr[0:128, fc:fc + 1],
            xpb[0:1, fc * 128:(fc + 1) * 128], core.onesrow[0:1, 0:1],
            start=True, stop=True),
            reads=[xpb, core.onesrow], writes=[core.ptr])
    p.op("ve", lambda e: e.tensor_copy(
        out=xth.h[:, :].rearrange("p (f c) -> p f c", f=FC)[:, :, 7:8],
        in_=core.ptr.h[0:128, 0:FC].rearrange("p (f c) -> p f c", f=FC)),
        reads=[core.ptr], writes=[xth])

    for ob in range(4):
        which = ob % 2
        stream_wcols(core, W["W_rT"], 1024, ob * 256, (ob + 1) * 256, which,
                     colsum=True, cr_off=0, wb_off=0, gamma_col=(mucol, 0))
        stream_wcols(core, W["W_rT"], 1024, ob * 256, (ob + 1) * 256, which,
                     colsum=True, cr_off=256, wb_off=2048, gamma_col=(mucol, 8))
        wb = core.wbf[which]
        cr = core.colr[which]
        for i in range(NT):
            pp = core.bank()
            pairs = []
            for k in range(FC):
                pairs.append((xth[:, k * S1 + 8 + i * 128:k * S1 + 8 + (i + 1) * 128],
                              wb[:, k * 256:(k + 1) * 256]))
                pairs.append((xth[:, k * S1 + 7 + i * 128:k * S1 + 7 + (i + 1) * 128],
                              wb[:, 2048 + k * 256:2048 + (k + 1) * 256]))
            mm_chain(p, pp[:, 0:256], pairs,
                     reads=[xth, wb, cr, rowsh, mucol], writes=[pp],
                     extras=[(rowsh[0:1, 8 + i * 128:8 + (i + 1) * 128], cr[0:1, 0:256]),
                             (rowsh[0:1, 7 + i * 128:7 + (i + 1) * 128], cr[0:1, 256:512])])
            p.op("se", lambda e, i=i, ob=ob, pp=pp: e.activation(
                out=rsig[:, i * 1024 + ob * 256:i * 1024 + (ob + 1) * 256],
                in_=pp[:, 0:256], func=AF.Sigmoid), reads=[pp], writes=[rsig])

    for oc in range(FC):
        which = oc % 2
        stream_wcols(core, W["W_kT"], 1024, oc * 128, (oc + 1) * 128, which,
                     colsum=True, cr_off=0, wb_off=0, gamma_col=(mucol, 16))
        stream_wcols(core, W["W_kT"], 1024, oc * 128, (oc + 1) * 128, which,
                     colsum=True, cr_off=128, wb_off=1024, gamma_col=(mucol, 24))
        wb = core.wbf[which]
        cr = core.colr[which]
        for s in range(NS):
            pp = core.bank()
            pairs = []
            for k in range(FC):
                pairs.append((wb[:, k * 128:(k + 1) * 128],
                              xth[:, k * S1 + 8 + s * 512:k * S1 + 8 + s * 512 + 512]))
                pairs.append((wb[:, 1024 + k * 128:1024 + (k + 1) * 128],
                              xth[:, k * S1 + 7 + s * 512:k * S1 + 7 + s * 512 + 512]))
            mm_chain(p, pp[:, 0:512], pairs,
                     reads=[xth, wb, cr, rowsh, mucol], writes=[pp],
                     extras=[(cr[0:1, 0:128], rowsh[0:1, 8 + s * 512:8 + s * 512 + 512]),
                             (cr[0:1, 128:256], rowsh[0:1, 7 + s * 512:7 + s * 512 + 512])])
            p.op("ve", lambda e, pp=pp: e.tensor_scalar(
                out=scr[:, 0:512], in0=pp[:, 0:512], scalar1=0.0, scalar2=None,
                op0=ALU.max), reads=[pp], writes=[scr])
            p.op("se", lambda e, oc=oc, s=s: e.activation(
                out=ka[:, oc * TOK + s * 512:oc * TOK + (s + 1) * 512],
                in_=scr[:, 0:512], func=AF.Square), reads=[scr], writes=[ka])

    xnew = scr
    for ob2 in range(2):
        which = ob2 % 2
        stream_wcols(core, W["W_vT"], 1024, ob2 * 512, (ob2 + 1) * 512, which,
                     colsum=False)
        wb = core.wbf[which]
        for i in range(NT):
            buf = core.xld[0]
            p.dma("sy", buf[:, :], x_src.ap()[i * 128:(i + 1) * 128, :],
                  reads=[x_src], writes=[buf])
            pp = core.bank()
            mm_chain(p, pp[:, 0:512],
                     [(ka[:, k * TOK + i * 128:k * TOK + (i + 1) * 128],
                       wb[:, k * 512:(k + 1) * 512]) for k in range(FC)],
                     reads=[ka, wb], writes=[pp])
            p.op("ve", lambda e, i=i, ob2=ob2, pp=pp: e.tensor_tensor(
                out=xnew[:, 0:512], in0=pp[:, 0:512],
                in1=rsig[:, i * 1024 + ob2 * 512:i * 1024 + (ob2 + 1) * 512],
                op=ALU.mult), reads=[pp, rsig], writes=[xnew])
            p.op("ve", lambda e, i=i, pp=pp, buf=buf, ob2=ob2: e.scalar_tensor_tensor(
                out=xnew[:, 512:1024], in0=xnew[:, 0:512], scalar=0.5,
                in1=buf[:, ob2 * 512:(ob2 + 1) * 512],
                op0=ALU.mult, op1=ALU.add), reads=[xnew, buf], writes=[xnew])
            p.dma("gp", x_dst.ap()[i * 128:(i + 1) * 128, ob2 * 512:(ob2 + 1) * 512],
                  xnew[:, 512:1024], reads=[xnew], writes=[x_dst], is_output=True)


L1FFN_W = dict(W_rT=[1024, 1024], W_kT=[1024, 1024], W_vT=[1024, 1024],
               mu_r=[1024], mu_k=[1024])


def build_l1_ffn_program():
    p = Prog("l1ffn")
    x_in = p.dram("x_in", [TOK, C], F32, "ExternalInput")
    xprev = p.dram("xprev", [1, C], F32, "ExternalInput")
    x_out = p.dram("x_out", [TOK, C], F32, "ExternalOutput")
    cident = p.dram("cident", [128, 128], F32, "ExternalInput")
    W = {k: p.dram(k, v, F32, "ExternalInput") for k, v in L1FFN_W.items()}
    core = Core(p)
    emit_load_ident(core, cident)
    emit_l1_ffn(core, W, x_in, x_out, xprev)
    return p.build()


def host_l1_ffn_weights(inp):
    T = lambda a: np.ascontiguousarray(np.asarray(a).T.astype(np.float32))
    return dict(W_rT=T(inp["L1_ffn_W_r"]), W_kT=T(inp["L1_ffn_W_k"]),
                W_vT=T(inp["L1_ffn_W_v"]),
                mu_r=np.asarray(inp["L1_ffn_mu_r"], np.float32),
                mu_k=np.asarray(inp["L1_ffn_mu_k"], np.float32))



def emit_l3_ffn(core, W, x_src, x_dst):
    p = core.p
    xt = p.sbuf("l3xt", [128, FC * TOK], BF16)
    big = p.sbuf("l3big", [128, 16 * TOK], BF16)
    oacc = p.sbuf("l3oacc", [128, FC * TOK], F32)
    hstg = p.sbuf("l3hstg", [128, 512], BF16)
    scr = p.sbuf("l3scr", [128, 1024], F32)
    sc3 = W["_sc3"]

    emit_stats(core, x_src)
    emit_xt(core, x_src, xt)

    for h in range(4):
        yh = big
        for oc in range(FC):
            which = oc % 2
            stream_wcols(core, W["bd1"], 128, 0, 128, which, colsum=True,
                         row0=h * 1024 + oc * 128)
            wb = core.wbf[which]
            cr = core.colr[which]
            for s in range(NS):
                pp = core.bank()
                mm_chain(p, pp[:, 0:512],
                         [(wb[:, 0:128], xt[:, oc * TOK + s * 512:oc * TOK + (s + 1) * 512])],
                         reads=[wb, cr, core.rows, xt], writes=[pp],
                         extras=[(cr[0:1, 0:128], core.rows[0:1, s * 512:(s + 1) * 512])])
                p.op("ve", lambda e, oc=oc, s=s, pp=pp: e.tensor_copy(
                    out=yh[:, oc * TOK + s * 512:oc * TOK + (s + 1) * 512],
                    in_=pp[:, 0:512]), reads=[pp], writes=[yh])
        for kc in range(32):
            jc = kc // 4
            which = kc % 2
            stream_wcols(core, W["bd2"], 128, 0, 128, which, colsum=False,
                         row0=(h * 32 + kc) * 128)
            wb = core.wbf[which]
            for s in range(NS):
                pp = core.bank()
                mm_chain(p, pp[:, 0:512],
                         [(wb[:, 0:128], yh[:, jc * TOK + s * 512:jc * TOK + (s + 1) * 512])],
                         reads=[wb, yh], writes=[pp])
                p.op("se", lambda e, pp=pp: e.activation(
                    out=hstg[:, :], in_=pp[:, 0:512], func=AF.Gelu),
                    reads=[pp], writes=[hstg])
                p.dma("gp", sc3.ap()[(h * 32 + kc) * 128:(h * 32 + kc + 1) * 128,
                                     s * 512:(s + 1) * 512],
                      hstg[:, :], reads=[hstg], writes=[sc3])

    for step in range(16):
        for k in range(8):
            p.dma("sy", big[:, k * TOK:(k + 1) * TOK],
                  sc3.ap()[(step * 8 + k) * 128:(step * 8 + k + 1) * 128, :],
                  reads=[sc3], writes=[big])
        for oc in range(FC):
            which = oc % 2
            stream_wcols(core, W["cpT"], 1024, oc * 128, (oc + 1) * 128, which,
                         colsum=False, row0=step * 1024)
            wb = core.wbf[which]
            for s in range(NS):
                pp = core.bank()
                mm_chain(p, pp[:, 0:512],
                         [(wb[:, k * 128:(k + 1) * 128],
                           big[:, k * TOK + s * 512:k * TOK + (s + 1) * 512]) for k in range(8)],
                         reads=[wb, big], writes=[pp])
                dst = oacc.h[:, oc * TOK + s * 512:oc * TOK + (s + 1) * 512]
                if step == 0:
                    p.op("ve", lambda e, dst=dst, pp=pp: e.tensor_copy(
                        out=dst, in_=pp[:, 0:512]), reads=[pp], writes=[oacc])
                else:
                    p.op("ve", lambda e, dst=dst, pp=pp: e.tensor_tensor(
                        out=dst, in0=pp[:, 0:512], in1=dst, op=ALU.add),
                        reads=[pp, oacc], writes=[oacc])

    emit_epilogue_fm32(core, oacc, x_src, x_dst, 0.5, scr)


def emit_epilogue_fm32(core, ofm, x_src, x_dst, coef, xnew, extra_fn=None):
    p = core.p
    for i in range(NT):
        buf = core.xld[i % 2]
        p.dma("sy", buf[:, :], x_src.ap()[i * 128:(i + 1) * 128, :],
              reads=[x_src], writes=[buf])
        for h in range(2):
            def trf(e, i=i, h=h):
                r = None
                for k in range(4):
                    fc = h * 4 + k
                    r = e.transpose(core.ptr[:, k * 128:(k + 1) * 128],
                                    ofm[:, fc * TOK + i * 128:fc * TOK + (i + 1) * 128],
                                    core.ident[:, :])
                return r
            p.op("pe", trf, reads=[ofm, core.ident], writes=[core.ptr])
            for k in range(4):
                fc = h * 4 + k
                p.op("ve", lambda e, i=i, fc=fc, k=k, buf=buf: e.scalar_tensor_tensor(
                    out=xnew[:, fc * 128 % 1024:fc * 128 % 1024 + 128],
                    in0=core.ptr[:, k * 128:(k + 1) * 128], scalar=coef,
                    in1=buf[:, fc * 128:(fc + 1) * 128],
                    op0=ALU.mult, op1=ALU.add), reads=[core.ptr, buf], writes=[xnew])
        p.dma("gp", x_dst.ap()[i * 128:(i + 1) * 128, :], xnew[:, :],
              reads=[xnew], writes=[x_dst], is_output=True)
        if extra_fn is not None:
            extra_fn(i)


L3FFN_W = dict(bd1=[4 * 8 * 128, 128], bd2=[4 * 32 * 128, 128], cpT=[16384, 1024])


def build_l3_ffn_program():
    p = Prog("l3ffn")
    x_in = p.dram("x_in", [TOK, C], F32, "ExternalInput")
    x_out = p.dram("x_out", [TOK, C], F32, "ExternalOutput")
    cident = p.dram("cident", [128, 128], F32, "ExternalInput")
    W = {k: p.dram(k, v, F32, "ExternalInput") for k, v in L3FFN_W.items()}
    W["_sc3"] = p.dram("sc3", [16384, TOK], BF16)
    core = Core(p)
    emit_load_ident(core, cident)
    emit_l3_ffn(core, W, x_in, x_out)
    return p.build()


def host_l3_ffn_weights(inp):
    RM = np.asarray(inp["L3_ffn_rail_maker"], np.float32)
    UP = np.asarray(inp["L3_ffn_rail_up_proj"], np.float32)
    bd1 = np.zeros((4, 8, 128, 128), np.float32)
    for h in range(4):
        for r in range(64):
            oc, ro = divmod(r, 8)
            bd1[h, oc, ro * 16:(ro + 1) * 16, ro * 16:(ro + 1) * 16] = RM[h, r]
    bd2 = np.zeros((4, 32, 128, 128), np.float32)
    for h in range(4):
        for kc in range(32):
            for sub in range(2):
                r = kc * 2 + sub
                jrow = (r % 8) * 16
                bd2[h, kc, jrow:jrow + 16, sub * 64:(sub + 1) * 64] = UP[h, r].T
    return dict(bd1=bd1.reshape(4 * 8 * 128, 128),
                bd2=bd2.reshape(4 * 32 * 128, 128),
                cpT=np.ascontiguousarray(np.asarray(inp["L3_ffn_w_cproj"], np.float32).T))



def emit_l0_ffn(core, W, x_src, x_dst):
    p = core.p
    B0 = p.sbuf("l0B0", [128, 16384], BF16)
    B1 = p.sbuf("l0B1", [128, 16384], BF16)
    B2 = p.sbuf("l0B2", [128, 16384], BF16)
    B3 = p.sbuf("l0B3", [128, 16384], BF16)
    l0st = p.sbuf("l0st", [128, 64], F32)
    mes = p.sbuf("l0mes", [128, 64], F32)
    g12 = p.sbuf("l0g12", [128, 32], F32)
    g12b = p.sbuf("l0g12b", [128, 32], BF16)
    csb = p.sbuf("l0csb", [128, 4], F32)
    mesb = p.sbuf("l0mesb", [128, TOK], F32)
    scr = p.sbuf("l0scr", [128, 1024], F32)
    stg = p.sbuf("l0stg", [128, 512], BF16)
    sc_x1 = W["_sc_x1"]
    sc_state = W["_sc_state"]

    load_fmcol(core, g12, W["ln1_w"], 1024, 0)
    load_fmcol(core, g12, W["ln1_b"], 1024, 8)
    load_fmcol(core, g12, W["ln2_w"], 1024, 16)
    load_fmcol(core, g12, W["ln2_b"], 1024, 24)
    p.op("ve", lambda e: e.tensor_copy(out=g12b[:, :], in_=g12[:, :]),
         reads=[g12], writes=[g12b])
    p.dma("sy", core.rowf[0:1, 0:3], W["cs"].ap()[0:3].rearrange("(a c) -> a c", a=1),
          reads=[W["cs"]], writes=[core.rowf])
    p.op("ve", lambda e: e.tensor_copy(out=stg[0:1, 0:3], in_=core.rowf[0:1, 0:3]),
         reads=[core.rowf], writes=[stg])
    p.op("pe", lambda e: e.matmul(core.pmisc[:, 0:3], core.onesrow[0:1, 0:128],
         stg[0:1, 0:3], start=True, stop=True), reads=[core.onesrow, stg],
         writes=[core.pmisc])
    p.op("ve", lambda e: e.tensor_copy(out=csb[:, 0:3], in_=core.pmisc[:, 0:3]),
         reads=[core.pmisc], writes=[csb])

    emit_stats(core, x_src)
    st = core.stats
    p.op("ve", lambda e: e.tensor_tensor(out=l0st[:, 0:16], in0=st[:, 16:32],
         in1=st[:, 48:64], op=ALU.mult), reads=[st], writes=[l0st])
    p.op("ve", lambda e: e.tensor_tensor(out=l0st[:, 0:16], in0=l0st[:, 0:16],
         in1=st[:, 48:64], op=ALU.mult), reads=[st, l0st], writes=[l0st])
    p.op("se", lambda e: e.activation(out=l0st[:, 0:16], in_=l0st[:, 0:16],
         func=AF.Ln, bias=core.epscol[:, 0:1]), reads=[l0st, core.epscol], writes=[l0st])
    p.op("se", lambda e: e.activation(out=l0st[:, 0:16], in_=l0st[:, 0:16],
         func=AF.Exp, scale=-0.5), reads=[l0st], writes=[l0st])
    p.op("ve", lambda e: e.tensor_tensor(out=l0st[:, 16:32], in0=l0st[:, 0:16],
         in1=st[:, 48:64], op=ALU.mult), reads=[st, l0st], writes=[l0st])
    p.op("ve", lambda e: e.tensor_tensor(out=l0st[:, 32:48], in0=l0st[:, 0:16],
         in1=st[:, 0:16], op=ALU.mult), reads=[st, l0st], writes=[l0st])
    p.op("ve", lambda e: e.tensor_copy(out=l0st[:, 48:64], in_=st[:, 48:64]),
         reads=[st], writes=[l0st])
    p.op("ve", lambda e: e.tensor_copy(out=st[:, 48:64], in_=l0st[:, 16:32]),
         reads=[l0st], writes=[st])
    p.op("ve", lambda e: e.tensor_copy(out=st[:, 0:16], in_=l0st[:, 32:48]),
         reads=[l0st], writes=[st])
    emit_rows(core, st, 0, 48)
    xt2 = B2
    emit_xt(core, x_src, xt2, rstd_col=48, gamma=(g12, 0))

    rsig, gv = B1, B0
    for wname, dstb, actfn in (("Wr", rsig, "sig"), ("Wv", gv, "copy")):
        for oc in range(FC):
            which = oc % 2
            stream_wcols(core, W[wname], 1024, oc * 128, (oc + 1) * 128, which,
                         colsum=False, colsum_g=(g12b, 0), bias_g=(g12b, 8), cr_off=0)
            wb = core.wbf[which]
            cr = core.colr[which]
            cb = core.colb[which]
            for s in range(NS):
                pp = core.bank()
                sl = slice(s * 512, (s + 1) * 512)
                mm_chain(p, pp[:, 0:512],
                         [(wb[:, k * 128:(k + 1) * 128],
                           xt2[:, k * TOK + s * 512:k * TOK + (s + 1) * 512]) for k in range(FC)],
                         reads=[wb, cr, cb, core.rows, core.onesrow, xt2], writes=[pp],
                         extras=[(cr[0:1, 0:128], core.rows[0:1, sl]),
                                 (cb[0:1, 0:128], core.onesrow[0:1, 0:512])])
                if actfn == "sig":
                    p.op("se", lambda e, oc=oc, s=s, pp=pp, dstb=dstb: e.activation(
                        out=dstb[:, oc * TOK + s * 512:oc * TOK + (s + 1) * 512],
                        in_=pp[:, 0:512], func=AF.Sigmoid), reads=[pp], writes=[dstb])
                else:
                    p.op("ve", lambda e, oc=oc, s=s, pp=pp, dstb=dstb: e.tensor_tensor(
                        out=dstb[:, oc * TOK + s * 512:oc * TOK + (s + 1) * 512],
                        in0=pp[:, 0:512],
                        in1=rsig[:, oc * TOK + s * 512:oc * TOK + (s + 1) * 512],
                        op=ALU.mult), reads=[pp, rsig], writes=[dstb])
    for oc in range(FC):
        which = oc % 2
        stream_wcols(core, W["Ws"], 1024, oc * 128, (oc + 1) * 128, which,
                     colsum=False, colsum_g=(g12b, 0), bias_g=(g12b, 8), cr_off=0)
        wb = core.wbf[which]
        cr = core.colr[which]
        cb = core.colb[which]
        for s in range(NS):
            pp = core.bank()
            sl = slice(s * 512, (s + 1) * 512)
            mm_chain(p, pp[:, 0:512],
                     [(wb[:, k * 128:(k + 1) * 128],
                       xt2[:, k * TOK + s * 512:k * TOK + (s + 1) * 512]) for k in range(FC)],
                     reads=[wb, cr, cb, core.rows, core.onesrow, xt2], writes=[pp],
                     extras=[(cr[0:1, 0:128], core.rows[0:1, sl]),
                             (cb[0:1, 0:128], core.onesrow[0:1, 0:512])])
            p.op("ve", lambda e, pp=pp: e.tensor_copy(out=stg[:, :], in_=pp[:, 0:512]),
                 reads=[pp], writes=[stg])
            p.dma("gp", sc_state.ap()[oc * 128:(oc + 1) * 128, s * 512:(s + 1) * 512],
                  stg[:, :], reads=[stg], writes=[sc_state])
    for ob in range(2):
        which = ob % 2
        stream_wcols(core, W["Wo"], 1024, ob * 512, (ob + 1) * 512, which, colsum=False)
        wb = core.wbf[which]
        for i in range(NT):
            buf = core.xld[0]
            p.dma("sy", buf[:, :], x_src.ap()[i * 128:(i + 1) * 128, :],
                  reads=[x_src], writes=[buf])
            pp = core.bank()
            mm_chain(p, pp[:, 0:512],
                     [(gv[:, k * TOK + i * 128:k * TOK + (i + 1) * 128],
                       wb[:, k * 512:(k + 1) * 512]) for k in range(FC)],
                     reads=[gv, wb], writes=[pp])
            p.op("ve", lambda e, i=i, ob=ob, buf=buf: e.tensor_scalar(
                out=scr[:, 0:512], in0=buf[:, ob * 512:(ob + 1) * 512],
                scalar1=st[:, 32 + i:33 + i], scalar2=l0st[:, 48 + i:49 + i],
                op0=ALU.subtract, op1=ALU.mult), reads=[buf, st, l0st], writes=[scr])
            p.op("ve", lambda e, pp=pp: e.tensor_tensor(
                out=scr[:, 512:1024], in0=pp[:, 0:512], in1=scr[:, 0:512],
                op=ALU.add), reads=[pp, scr], writes=[scr])
            p.dma("gp", sc_x1.ap()[i * 128:(i + 1) * 128, ob * 512:(ob + 1) * 512],
                  scr[:, 512:1024], reads=[scr], writes=[sc_x1])

    emit_stats(core, sc_x1)
    xt3 = B2
    emit_xt(core, sc_x1, xt3, rstd_col=48, gamma=(g12, 16))

    for i in range(NT):
        which = i % 2
        if i == 0:
            stream_wcols(core, W["Ppack"], 1024, 0, 8, 0, colsum=False,
                         colsum_g=(g12b, 16), bias_g=(g12b, 24), cr_off=0)
        wb = core.wbf[0]
        cr = core.colr[0]
        cb = core.colb[0]
        pp = core.pmisc
        mm_chain(p, pp[:, 0:8],
                 [(xt3[:, k * TOK + i * 128:k * TOK + (i + 1) * 128],
                   wb[:, k * 8:(k + 1) * 8]) for k in range(FC)],
                 reads=[wb, cr, cb, core.rows, core.onesrow, xt3], writes=[pp],
                 extras=[(core.rows[0:1, i * 128:(i + 1) * 128], cr[0:1, 0:8]),
                         (core.onesrow[0:1, 0:128], cb[0:1, 0:8])])
        p.op("se", lambda e, pp=pp: e.activation(out=scr[:, 0:4], in_=pp[:, 0:4],
             func=AF.Sigmoid), reads=[pp], writes=[scr])
        p.op("ve", lambda e, pp=pp: e.tensor_copy(out=scr[:, 4:7], in_=pp[:, 4:7]),
             reads=[pp], writes=[scr])
        p.op("ve", lambda e: e.tensor_tensor(out=scr[:, 8:11], in0=scr[:, 0:3],
             in1=csb[:, 0:3], op=ALU.mult), reads=[scr, csb], writes=[scr])
        p.op("ve", lambda e: e.tensor_scalar(out=scr[:, 8:11], in0=scr[:, 8:11],
             scalar1=scr[:, 3:4], scalar2=None, op0=ALU.mult), reads=[scr], writes=[scr])
        p.op("ve", lambda e: e.tensor_tensor(out=scr[:, 8:11], in0=scr[:, 8:11],
             in1=scr[:, 4:7], op=ALU.add), reads=[scr], writes=[scr])
        p.op("ve", lambda e: e.tensor_tensor(out=scr[:, 16:17], in0=scr[:, 8:9],
             in1=scr[:, 9:10], op=ALU.is_ge), reads=[scr], writes=[scr])
        p.op("ve", lambda e: e.tensor_tensor(out=scr[:, 17:18], in0=scr[:, 8:9],
             in1=scr[:, 10:11], op=ALU.is_ge), reads=[scr], writes=[scr])
        p.op("ve", lambda e: e.tensor_tensor(out=scr[:, 12:13], in0=scr[:, 16:17],
             in1=scr[:, 17:18], op=ALU.mult), reads=[scr], writes=[scr])
        p.op("ve", lambda e: e.tensor_tensor(out=scr[:, 16:17], in0=scr[:, 9:10],
             in1=scr[:, 8:9], op=ALU.is_gt), reads=[scr], writes=[scr])
        p.op("ve", lambda e: e.tensor_tensor(out=scr[:, 17:18], in0=scr[:, 9:10],
             in1=scr[:, 10:11], op=ALU.is_ge), reads=[scr], writes=[scr])
        p.op("ve", lambda e: e.tensor_tensor(out=scr[:, 13:14], in0=scr[:, 16:17],
             in1=scr[:, 17:18], op=ALU.mult), reads=[scr], writes=[scr])
        p.op("ve", lambda e: e.tensor_tensor(out=scr[:, 16:17], in0=scr[:, 10:11],
             in1=scr[:, 8:9], op=ALU.is_gt), reads=[scr], writes=[scr])
        p.op("ve", lambda e: e.tensor_tensor(out=scr[:, 17:18], in0=scr[:, 10:11],
             in1=scr[:, 9:10], op=ALU.is_gt), reads=[scr], writes=[scr])
        p.op("ve", lambda e: e.tensor_tensor(out=scr[:, 14:15], in0=scr[:, 16:17],
             in1=scr[:, 17:18], op=ALU.mult), reads=[scr], writes=[scr])
        p.op("ve", lambda e: e.tensor_tensor(out=scr[:, 16:19:2] if False else scr[:, 20:23],
             in0=scr[:, 12:15], in1=scr[:, 0:3], op=ALU.mult), reads=[scr], writes=[scr])
        p.op("ve", lambda e: e.tensor_tensor(out=scr[:, 18:19], in0=scr[:, 20:21],
             in1=scr[:, 21:22], op=ALU.add), reads=[scr], writes=[scr])
        p.op("ve", lambda e: e.tensor_tensor(out=scr[:, 18:19], in0=scr[:, 18:19],
             in1=scr[:, 22:23], op=ALU.add), reads=[scr], writes=[scr])
        p.op("ve", lambda e: e.tensor_scalar(out=scr[:, 19:20], in0=scr[:, 18:19],
             scalar1=1e-6, scalar2=None, op0=ALU.add), reads=[scr], writes=[scr])
        p.op("ve", lambda e: e.reciprocal(out=scr[:, 19:20], in_=scr[:, 19:20]),
             reads=[scr], writes=[scr])
        p.op("ve", lambda e: e.tensor_tensor(out=scr[:, 19:20], in0=scr[:, 18:19],
             in1=scr[:, 19:20], op=ALU.mult), reads=[scr], writes=[scr])
        p.op("ve", lambda e, i=i: e.tensor_scalar(out=mes[:, i * 3:(i + 1) * 3],
             in0=scr[:, 12:15], scalar1=scr[:, 19:20], scalar2=None, op0=ALU.mult),
             reads=[scr], writes=[mes])

    p.op("ve", lambda e: e.memset(B3[:, :], 0.0), writes=[B3])
    final = B3
    for e_i in range(2):
        kname, vname = f"K{e_i}", f"V{e_i}"
        emit_mes_bcast(core, mes, e_i, mesb)
        for half in range(2):
            hr = (B0, B1)
            for hc in range(32):
                which = hc % 2
                stream_wcols(core, W[kname], 1024, hc * 128, (hc + 1) * 128, which,
                             colsum=False, colsum_g=(g12b, 16), bias_g=(g12b, 24))
                wb = core.wbf[which]
                cr = core.colr[which]
                cb = core.colb[which]
                for s2 in range(2):
                    s = half * 2 + s2
                    pp = core.bank()
                    sl = slice(s * 512, (s + 1) * 512)
                    mm_chain(p, pp[:, 0:512],
                             [(wb[:, k * 128:(k + 1) * 128],
                               xt3[:, k * TOK + s * 512:k * TOK + (s + 1) * 512]) for k in range(FC)],
                             reads=[wb, cr, cb, core.rows, core.onesrow, xt3],
                             writes=[pp],
                             extras=[(cr[0:1, 0:128], core.rows[0:1, sl]),
                                     (cb[0:1, 0:128], core.onesrow[0:1, 0:512])])
                    p.op("ve", lambda e, pp=pp: e.tensor_scalar(
                        out=scr[:, 0:512], in0=pp[:, 0:512], scalar1=0.0,
                        scalar2=None, op0=ALU.max), reads=[pp], writes=[scr])
                    hrb = hr[hc // 16]
                    c0 = (hc % 16) * 1024 + s2 * 512
                    p.op("se", lambda e, hrb=hrb, c0=c0: e.activation(
                        out=hrb[:, c0:c0 + 512], in_=scr[:, 0:512], func=AF.Square),
                        reads=[scr], writes=[hrb])
            for oc in range(FC):
                which = oc % 2
                stream_wcols(core, W[vname], 4096, oc * 128, (oc + 1) * 128, which,
                             colsum=False)
                wb = core.wbf[which]
                for s2 in range(2):
                    s = half * 2 + s2
                    pp = core.bank()
                    pairs = []
                    for hc in range(32):
                        hrb = hr[hc // 16]
                        c0 = (hc % 16) * 1024 + s2 * 512
                        pairs.append((wb[:, hc * 128:(hc + 1) * 128], hrb[:, c0:c0 + 512]))
                    mm_chain(p, pp[:, 0:512], pairs, reads=[wb, B0, B1], writes=[pp])
                    p.op("ve", lambda e, pp=pp, s=s: e.tensor_tensor(
                        out=scr[:, 0:512], in0=pp[:, 0:512],
                        in1=mesb[:, s * 512:(s + 1) * 512], op=ALU.mult),
                        reads=[pp, mesb], writes=[scr])
                    p.op("ve", lambda e, oc=oc, s=s: e.tensor_tensor(
                        out=final[:, oc * TOK + s * 512:oc * TOK + (s + 1) * 512],
                        in0=scr[:, 0:512],
                        in1=final[:, oc * TOK + s * 512:oc * TOK + (s + 1) * 512],
                        op=ALU.add), reads=[scr, final], writes=[final])

    emit_mes_bcast(core, mes, 2, mesb)
    statefm = B0
    for k in range(FC):
        p.dma("sy", statefm[:, k * TOK:(k + 1) * TOK],
              sc_state.ap()[k * 128:(k + 1) * 128, :], reads=[sc_state], writes=[statefm])
    sg = B1
    for oc in range(FC):
        which = oc % 2
        stream_wcols(core, W["W2"], 1024, oc * 128, (oc + 1) * 128, which, colsum=False)
        wb = core.wbf[which]
        for s in range(NS):
            pp = core.bank()
            mm_chain(p, pp[:, 0:512],
                     [(wb[:, k * 128:(k + 1) * 128],
                       statefm[:, k * TOK + s * 512:k * TOK + (s + 1) * 512]) for k in range(FC)],
                     reads=[wb, statefm], writes=[pp])
            p.op("se", lambda e, oc=oc, s=s, pp=pp: e.activation(
                out=sg[:, oc * TOK + s * 512:oc * TOK + (s + 1) * 512],
                in_=pp[:, 0:512], func=AF.Sigmoid), reads=[pp], writes=[sg])
    u1 = B0
    for oc in range(FC):
        which = oc % 2
        stream_wcols(core, W["W1"], 1024, oc * 128, (oc + 1) * 128, which,
                     colsum=False, colsum_g=(g12b, 16), bias_g=(g12b, 24))
        wb = core.wbf[which]
        cr = core.colr[which]
        cb = core.colb[which]
        for s in range(NS):
            pp = core.bank()
            sl = slice(s * 512, (s + 1) * 512)
            mm_chain(p, pp[:, 0:512],
                     [(wb[:, k * 128:(k + 1) * 128],
                       xt3[:, k * TOK + s * 512:k * TOK + (s + 1) * 512]) for k in range(FC)],
                     reads=[wb, cr, cb, core.rows, core.onesrow, xt3], writes=[pp],
                     extras=[(cr[0:1, 0:128], core.rows[0:1, sl]),
                             (cb[0:1, 0:128], core.onesrow[0:1, 0:512])])
            p.op("ve", lambda e, oc=oc, s=s, pp=pp: e.tensor_tensor(
                out=sg[:, oc * TOK + s * 512:oc * TOK + (s + 1) * 512],
                in0=pp[:, 0:512],
                in1=sg[:, oc * TOK + s * 512:oc * TOK + (s + 1) * 512],
                op=ALU.mult), reads=[pp, sg], writes=[sg])
    q = sg
    for oc in range(FC):
        which = oc % 2
        stream_wcols(core, W["W3"], 1024, oc * 128, (oc + 1) * 128, which, colsum=False)
        wb = core.wbf[which]
        for s in range(NS):
            pp = core.bank()
            mm_chain(p, pp[:, 0:512],
                     [(wb[:, k * 128:(k + 1) * 128],
                       q[:, k * TOK + s * 512:k * TOK + (s + 1) * 512]) for k in range(FC)],
                     reads=[wb, q], writes=[pp])
            p.op("ve", lambda e, pp=pp, s=s: e.tensor_tensor(
                out=scr[:, 0:512], in0=pp[:, 0:512],
                in1=mesb[:, s * 512:(s + 1) * 512], op=ALU.mult),
                reads=[pp, mesb], writes=[scr])
            p.op("ve", lambda e, oc=oc, s=s: e.tensor_tensor(
                out=final[:, oc * TOK + s * 512:oc * TOK + (s + 1) * 512],
                in0=scr[:, 0:512],
                in1=final[:, oc * TOK + s * 512:oc * TOK + (s + 1) * 512],
                op=ALU.add), reads=[scr, final], writes=[final])

    ptrb_ap = core.ptr.h[:, :].bitcast(BF16)
    x1ld = p.sbuf("l0x1ld", [128, 1024], F32)
    for i in range(NT):
        buf = core.xld[0]
        p.dma("sy", buf[:, :], x_src.ap()[i * 128:(i + 1) * 128, :],
              reads=[x_src], writes=[buf])
        p.dma("sy", x1ld[:, :], sc_x1.ap()[i * 128:(i + 1) * 128, :],
              reads=[sc_x1], writes=[x1ld])
        for h in range(2):
            def trf(e, i=i, h=h):
                r = None
                for k in range(4):
                    fc = h * 4 + k
                    r = e.transpose(ptrb_ap[:, k * 128:(k + 1) * 128],
                                    final[:, fc * TOK + i * 128:fc * TOK + (i + 1) * 128],
                                    core.identb[:, :])
                return r
            p.op("pe", trf, reads=[final, core.identb], writes=[core.ptr])
            for k in range(4):
                fc = h * 4 + k
                p.op("ve", lambda e, fc=fc, k=k: e.scalar_tensor_tensor(
                    out=scr[:, fc * 128:(fc + 1) * 128],
                    in0=ptrb_ap[:, k * 128:(k + 1) * 128], scalar=1.0,
                    in1=x1ld[:, fc * 128:(fc + 1) * 128],
                    op0=ALU.mult, op1=ALU.add), reads=[core.ptr, x1ld], writes=[scr])
        p.op("ve", lambda e: e.scalar_tensor_tensor(
            out=scr[:, 0:1024], in0=scr[:, 0:1024], scalar=0.5, in1=buf[:, :],
            op0=ALU.mult, op1=ALU.add), reads=[scr, buf], writes=[scr])
        p.dma("gp", x_dst.ap()[i * 128:(i + 1) * 128, :], scr[:, 0:1024],
              reads=[scr], writes=[x_dst], is_output=True)


def emit_mes_bcast(core, mes, e_i, mesb):
    p = core.p
    p.op("ve", lambda e: e.tensor_copy(
        out=core.tmp32[0:32, 0:64].rearrange("a b -> a b") if False else core.stats[:, 0:16],
        in_=mes.h[:, e_i:e_i + 48:3]), reads=[mes], writes=[core.stats])
    pk = core.dump
    p.op("ve", lambda e: e.tensor_copy(out=pk[:, 0:16], in_=core.stats[:, 0:16]),
         reads=[core.stats], writes=[pk])
    p.op("pe", lambda e: e.transpose(core.ptr[0:16, 0:128], pk[:, 0:16],
         core.ident[:, :]), reads=[pk, core.ident], writes=[core.ptr])
    p.op("ve", lambda e: e.tensor_copy(out=core.tmp32[0:16, :], in_=core.ptr[0:16, 0:128]),
         reads=[core.ptr], writes=[core.tmp32])
    p.dma("gp", core.rb_dram.ap()[0:16, :], core.tmp32.h[0:16, :],
          reads=[core.tmp32], writes=[core.rb_dram])
    p.dma("gp", core.rowf[0:1, :],
          core.rb_dram.ap()[0:16, :].rearrange("(a t) p -> a (t p)", a=1),
          reads=[core.rb_dram], writes=[core.rowf])
    for s in range(NS):
        p.op("pe", lambda e, s=s: e.matmul(core.pmisc[:, 0:512],
             core.onesf[0:1, 0:128],
             core.rowf[0:1, s * 512:(s + 1) * 512], start=True, stop=True),
             reads=[core.onesf, core.rowf], writes=[core.pmisc])
        p.op("ve", lambda e, s=s: e.tensor_copy(
            out=mesb[:, s * 512:(s + 1) * 512], in_=core.pmisc[:, 0:512]),
            reads=[core.pmisc], writes=[mesb])


L0FFN_W = dict(Wr=[1024, 1024], Wv=[1024, 1024], Wo=[1024, 1024], Ws=[1024, 1024],
               W1=[1024, 1024], W2=[1024, 1024], W3=[1024, 1024],
               K0=[1024, 4096], K1=[1024, 4096], V0=[4096, 1024], V1=[4096, 1024],
               Ppack=[1024, 8], ln1_w=[1024], ln1_b=[1024], ln2_w=[1024],
               ln2_b=[1024], cs=[3])


def build_l0_ffn_program():
    p = Prog("l0ffn")
    x_in = p.dram("x_in", [TOK, C], F32, "ExternalInput")
    x_out = p.dram("x_out", [TOK, C], F32, "ExternalOutput")
    cident = p.dram("cident", [128, 128], F32, "ExternalInput")
    W = {k: p.dram(k, v, F32, "ExternalInput") for k, v in L0FFN_W.items()}
    W["_sc_x1"] = p.dram("sc_x1", [TOK, C], F32)
    W["_sc_state"] = p.dram("sc_state", [1024, TOK], BF16)
    core = Core(p)
    emit_load_ident(core, cident)
    emit_l0_ffn(core, W, x_in, x_out)
    return p.build()


def host_l0_ffn_weights(inp):
    A = lambda k: np.asarray(inp[k], np.float32)
    Ppack = np.zeros((1024, 8), np.float32)
    Ppack[:, 0] = A("L0_ffn_conf_rwkv")[0]
    Ppack[:, 1] = A("L0_ffn_conf_rwkv")[1]
    Ppack[:, 2] = A("L0_ffn_conf_trans")
    Ppack[:, 3] = A("L0_ffn_w_diff")
    Ppack[:, 4:7] = A("L0_ffn_W_aff")
    return dict(Wr=A("L0_ffn_Wr"), Wv=A("L0_ffn_Wv"), Wo=A("L0_ffn_Wo"),
                Ws=A("L0_ffn_Ws"), W1=A("L0_ffn_W1"), W2=A("L0_ffn_W2"),
                W3=A("L0_ffn_W3"),
                K0=np.ascontiguousarray(A("L0_ffn_K_rwkv")[0]),
                K1=np.ascontiguousarray(A("L0_ffn_K_rwkv")[1]),
                V0=np.ascontiguousarray(A("L0_ffn_V_rwkv")[0]),
                V1=np.ascontiguousarray(A("L0_ffn_V_rwkv")[1]),
                Ppack=Ppack, ln1_w=A("L0_ffn_ln1_w"), ln1_b=A("L0_ffn_ln1_b"),
                ln2_w=A("L0_ffn_ln2_w"), ln2_b=A("L0_ffn_ln2_b"),
                cs=A("L0_ffn_capital_shares"))



SEQ = 8192
NKT = SEQ // 128


def emit_slab_stats(core, src, slab, nt=16, row0=None):
    p = core.p
    st = core.stats
    if row0 is None:
        row0 = slab * 2048
    p.op("ve", lambda e: e.memset(st[:, 0:32], 0.0), writes=[st])
    for i in range(nt):
        buf = core.xld[i % 2]
        p.dma("sy", buf[:, :], src.ap()[row0 + i * 128:row0 + (i + 1) * 128, :],
              reads=[src], writes=[buf])
        p.op("se", lambda e, i=i, buf=buf: e.activation(
            out=core.dump[:, 0:1024], in_=buf[:, :], func=AF.Copy,
            accum_out=st[:, i:i + 1]), reads=[buf], writes=[core.dump, st])
        p.op("se", lambda e, i=i, buf=buf: e.activation(
            out=core.dump[:, 0:1024], in_=buf[:, :], func=AF.Square,
            accum_out=st[:, 16 + i:17 + i]), reads=[buf], writes=[core.dump, st])
    finish_stats(core)


def emit_xt_rows(core, src, dst, slab, row0=None):
    if row0 is None:
        row0 = slab * 2048
    emit_slab_stats(core, src, slab, row0=row0)
    emit_xt_slab(core, src, dst, row0)


def emit_xt_slab(core, src, dst, row0):
    p = core.p
    st = core.stats
    for i in range(NT):
        buf = core.xld[i % 2]
        p.dma("sy", buf[:, :], src.ap()[row0 + i * 128:row0 + (i + 1) * 128, :],
              reads=[src], writes=[buf])
        p.op("ve", lambda e, i=i: e.tensor_scalar(
            out=core.diag[:, :], in0=core.ident[:, :],
            scalar1=st[:, 48 + i:49 + i], scalar2=None, op0=ALU.mult),
            reads=[core.ident, st], writes=[core.diag])
        for h in range(2):
            def mmfn(e, i=i, h=h, buf=buf):
                r = None
                for k in range(4):
                    fc = h * 4 + k
                    r = e.matmul(core.ptr[:, k * 128:(k + 1) * 128],
                                 buf[:, fc * 128:(fc + 1) * 128],
                                 core.diag[:, :], start=True, stop=True)
                return r
            p.op("pe", mmfn, reads=[buf, core.diag], writes=[core.ptr])
            for k in range(4):
                fc = h * 4 + k
                c0 = fc * TOK + i * 128
                p.op("ve", lambda e, c0=c0, k=k: e.tensor_copy(
                    out=dst[:, c0:c0 + 128], in_=core.ptr[:, k * 128:(k + 1) * 128]),
                    reads=[core.ptr], writes=[dst])


def emit_fmod(p, dst_ap, src_ap, N, scrf_ap, scri_ap, bufs_r, bufs_w, scr_buf):
    p.op("ve", lambda e: e.tensor_scalar(out=scrf_ap, in0=src_ap,
         scalar1=1.0 / N, scalar2=-0.5, op0=ALU.mult, op1=ALU.add),
         reads=bufs_r, writes=[scr_buf])
    p.op("ve", lambda e: e.tensor_copy(out=scri_ap, in_=scrf_ap),
         reads=[scr_buf], writes=[scr_buf])
    p.op("ve", lambda e: e.tensor_copy(out=scrf_ap, in_=scri_ap),
         reads=[scr_buf], writes=[scr_buf])
    p.op("ve", lambda e: e.scalar_tensor_tensor(out=dst_ap, in0=scrf_ap,
         scalar=float(-N), in1=src_ap, op0=ALU.mult, op1=ALU.add),
         reads=bufs_r + [scr_buf], writes=bufs_w)


def emit_trig(core, trig, scr2, pos0, invth_expr, sgn_period=None, npts=TOK, scr_off=4096,
              pos_col=None, pos_col_buf=None):
    p = core.p
    TWO_PI = float(2 * np.pi)
    it = core.diag
    iti = Buf(it.h, "it_alias")
    p.op("gp", lambda e: e.iota(it[:, 2:3], pattern=[[0, 1]], base=0,
         channel_multiplier=1, allow_small_or_imprecise_dtypes=True), writes=[it])
    emit_fmod(p, it[:, 0:1], it[:, 2:3], 32.0, it[:, 9:10],
              it.h[:, 9:10].bitcast(mybir.dt.int32), [it], [it], it)
    invth_expr(p, core, it)
    if sgn_period == 64:
        emit_fmod(p, it[:, 1:2], it[:, 2:3], 64.0, it[:, 9:10],
                  it.h[:, 9:10].bitcast(mybir.dt.int32), [it], [it], it)
        p.op("ve", lambda e: e.tensor_scalar(out=it[:, 1:2], in0=it[:, 1:2],
             scalar1=32.0, scalar2=2.0, op0=ALU.is_lt, op1=ALU.mult),
             reads=[it], writes=[it])
        p.op("ve", lambda e: e.tensor_scalar(out=it[:, 1:2], in0=it[:, 1:2],
             scalar1=-1.0, scalar2=None, op0=ALU.add), reads=[it], writes=[it])
    else:
        p.op("ve", lambda e: e.memset(it[:, 1:2], 1.0), writes=[it])
    p.op("gp", lambda e: e.iota(scr2.h[:, scr_off:scr_off + npts], pattern=[[1, npts]], base=pos0,
         channel_multiplier=0, allow_small_or_imprecise_dtypes=True), writes=[scr2])
    if pos_col is not None:
        p.op("ve", lambda e: e.tensor_scalar(out=scr2.h[:, scr_off:scr_off + npts],
             in0=scr2.h[:, scr_off:scr_off + npts], scalar1=pos_col, scalar2=None,
             op0=ALU.add), reads=[scr2, pos_col_buf], writes=[scr2])
    p.op("ve", lambda e: e.tensor_scalar(out=scr2.h[:, scr_off:scr_off + npts], in0=scr2.h[:, scr_off:scr_off + npts],
         scalar1=it[:, 0:1], scalar2=None, op0=ALU.mult),
         reads=[scr2, it], writes=[scr2])
    for which, shift in ((0, 0.5 * np.pi), (1, 0.0)):
        dst0 = (0 if which == 0 else npts)
        p.op("ve", lambda e, shift=shift, dst0=dst0: e.tensor_scalar(
            out=trig[:, dst0:dst0 + npts], in0=scr2.h[:, scr_off:scr_off + npts],
            scalar1=float(shift), scalar2=None, op0=ALU.add),
            reads=[scr2], writes=[trig])
        emit_fmod(p, trig.h[:, dst0:dst0 + npts], trig.h[:, dst0:dst0 + npts],
                  TWO_PI, scr2.h[:, scr_off + npts:scr_off + 2 * npts],
                  scr2.h[:, scr_off + npts:scr_off + 2 * npts].bitcast(mybir.dt.int32),
                  [trig], [trig], scr2)
        p.op("ve", lambda e, dst0=dst0: e.tensor_scalar(
            out=scr2.h[:, scr_off + npts:scr_off + 2 * npts], in0=trig[:, dst0:dst0 + npts],
            scalar1=float(np.pi), scalar2=-TWO_PI, op0=ALU.is_ge, op1=ALU.mult),
            reads=[trig], writes=[scr2])
        p.op("ve", lambda e, dst0=dst0: e.tensor_tensor(
            out=trig[:, dst0:dst0 + npts], in0=trig[:, dst0:dst0 + npts],
            in1=scr2.h[:, scr_off + npts:scr_off + 2 * npts], op=ALU.add), reads=[trig, scr2], writes=[trig])
        p.op("se", lambda e, dst0=dst0: e.activation(
            out=trig[:, dst0:dst0 + npts], in_=trig[:, dst0:dst0 + npts],
            func=AF.Sin), reads=[trig], writes=[trig])
    p.op("ve", lambda e: e.tensor_scalar(out=trig[:, npts:2 * npts],
         in0=trig[:, npts:2 * npts], scalar1=it[:, 1:2], scalar2=None,
         op0=ALU.mult), reads=[trig, it], writes=[trig])



def _l3_invth(p, core, it):
    p.op("ve", lambda e: e.tensor_scalar(out=it[:, 0:1], in0=it[:, 0:1],
         scalar1=float(-np.log(10000.0) / 31.0), scalar2=None, op0=ALU.mult),
         reads=[it], writes=[it])
    p.op("se", lambda e: e.activation(out=it[:, 0:1], in_=it[:, 0:1], func=AF.Exp),
         reads=[it], writes=[it])


def emit_rope_block(core, stage, trig, tcol0, width, rotm, dst_ap, dst_buf, p):
    pp = core.bank()
    mm_chain(p, pp[:, 0:width], [(rotm[:, :], stage[:, 0:width])],
             reads=[rotm, stage], writes=[pp])
    p.op("ve", lambda e: e.tensor_tensor(out=core.dump[:, 0:width], in0=pp[:, 0:width],
         in1=trig[:, 2048 + tcol0:2048 + tcol0 + width], op=ALU.mult),
         reads=[pp, trig], writes=[core.wstage[0]])
    p.op("ve", lambda e: e.scalar_tensor_tensor(
        out=core.dump[:, 512:512 + width] if width <= 512 else None,
        in0=stage[:, 0:width], scalar=1.0,
        in1=trig[:, tcol0:tcol0 + width], op0=ALU.bypass, op1=ALU.mult),
        reads=[stage, trig], writes=[core.wstage[0]])
    p.op("ve", lambda e: e.tensor_tensor(out=dst_ap, in0=core.dump[:, 0:width],
         in1=core.dump[:, 512:512 + width], op=ALU.add),
         reads=[core.wstage[0]], writes=[dst_buf])


def emit_qoffcol(core, W):
    p = core.p
    qv = p.sbuf("qoffv_sb", [1, 1], F32)
    qc = p.sbuf("qoffcol", [128, 1], F32)
    p.dma("sy", qv[0:1, 0:1], W["qoffv"].ap()[0:1, 0:1], reads=[W["qoffv"]], writes=[qv])
    p.op("pe", lambda e: e.matmul(core.pmisc[:, 30:31], core.onesf[0:1, 0:128],
         qv[0:1, 0:1], start=True, stop=True), reads=[core.onesf, qv],
         writes=[core.pmisc])
    p.op("ve", lambda e: e.tensor_copy(out=qc[:, :], in_=core.pmisc[:, 30:31]),
         reads=[core.pmisc], writes=[qc])
    return qc


def emit_l3_attn(core, W, x_src, x_dst, xf, qoff):
    p = core.p
    xt = p.sbuf("l3a_xt", [128, FC * TOK], BF16)
    qdm = p.sbuf("l3a_q", [128, FC * TOK], BF16)
    acc = p.sbuf("l3a_acc", [128, 16384], F32)
    trig = Buf(acc.h, "l3a_trig")
    p.bufs.append(trig)
    trig_alias = trig
    trigscr = Buf(acc.h, "l3a_trigscr")
    p.bufs.append(trigscr)
    kslab = p.sbuf("l3a_ks", [128, 2048], BF16)
    vslab = p.sbuf("l3a_vs", [128, 4096], BF16)
    pstage = p.sbuf("l3a_ps", [128, 512], BF16)
    ptst = p.sbuf("l3a_pt", [128, 512], BF16)
    nml = p.sbuf("l3a_nml", [128, 512], F32)
    bmask = p.sbuf("l3a_bm", [128, 8192], BF16)
    rotm = p.sbuf("l3a_rot", [128, 128], BF16)
    stg = p.sbuf("l3a_stg", [128, 512], BF16)
    kd, vd = W["_kd"], W["_vd"]
    qoffcol = emit_qoffcol(core, W)

    p.dma("sy", core.dump[:, 0:128], W["rotm"].ap()[:, :], reads=[W["rotm"]],
          writes=[core.wstage[0]])
    p.op("ve", lambda e: e.tensor_copy(out=rotm[:, :], in_=core.dump[:, 0:128]),
         reads=[core.wstage[0]], writes=[rotm])

    for slab in range(4):
        emit_xt_rows(core, xf, xt, slab)
        emit_trig(core, trig_alias, trigscr, slab * 2048, _l3_invth, sgn_period=64)
        for oc in range(FC):
            which = oc % 2
            stream_wcols(core, W["W_inT"], 1024, 1024 + oc * 128, 1024 + (oc + 1) * 128,
                         which, colsum=True, bias=W["b_in"])
            wb = core.wbf[which]
            cr = core.colr[which]
            cb = core.colb[which]
            for s in range(NS):
                pp = core.bank()
                sl = slice(s * 512, (s + 1) * 512)
                mm_chain(p, pp[:, 0:512],
                         [(wb[:, k * 128:(k + 1) * 128],
                           xt[:, k * TOK + s * 512:k * TOK + (s + 1) * 512]) for k in range(FC)],
                         reads=[wb, cr, cb, core.rows, core.onesrow, xt], writes=[pp],
                         extras=[(cr[0:1, 0:128], core.rows[0:1, sl]),
                                 (cb[0:1, 0:128], core.onesrow[0:1, 0:512])])
                p.op("ve", lambda e, pp=pp: e.tensor_copy(out=stg[:, :], in_=pp[:, 0:512]),
                     reads=[pp], writes=[stg])
                emit_rope_block(core, stg, trig_alias, s * 512, 512, rotm,
                                stg[:, 0:512], stg, p)
                p.dma("gp", kd.ap()[oc * 128:(oc + 1) * 128,
                                    slab * 2048 + s * 512:slab * 2048 + (s + 1) * 512],
                      stg[:, :], reads=[stg], writes=[kd])
        for ob in range(2):
            which = ob % 2
            stream_wcols(core, W["W_inT"], 1024, 2048 + ob * 512, 2048 + (ob + 1) * 512,
                         which, colsum=True, bias=W["b_in"])
            wb = core.wbf[which]
            cr = core.colr[which]
            cb = core.colb[which]
            for i in range(NT):
                pp = core.bank()
                mm_chain(p, pp[:, 0:512],
                         [(xt[:, k * TOK + i * 128:k * TOK + (i + 1) * 128],
                           wb[:, k * 512:(k + 1) * 512]) for k in range(FC)],
                         reads=[wb, cr, cb, core.rows, core.onesrow, xt], writes=[pp],
                         extras=[(core.rows[0:1, i * 128:(i + 1) * 128], cr[0:1, 0:512]),
                                 (core.onesrow[0:1, 0:128], cb[0:1, 0:512])])
                p.op("ve", lambda e, pp=pp: e.tensor_copy(out=stg[:, :], in_=pp[:, 0:512]),
                     reads=[pp], writes=[stg])
                p.dma("gp", vd.ap()[slab * 2048 + i * 128:slab * 2048 + (i + 1) * 128,
                                    ob * 512:(ob + 1) * 512],
                      stg[:, :], reads=[stg], writes=[vd])

    emit_xt_rows(core, x_src, xt, 0, row0=0)
    emit_trig(core, trig_alias, trigscr, 0, _l3_invth, sgn_period=64,
              pos_col=qoffcol[:, 0:1], pos_col_buf=qoffcol)
    for half in range(2):
        p.op("ve", lambda e, half=half: e.tensor_scalar(
            out=trig_alias[:, half * 2048:(half + 1) * 2048],
            in0=trig_alias[:, half * 2048:(half + 1) * 2048],
            scalar1=0.125, scalar2=None, op0=ALU.mult),
            reads=[trig_alias], writes=[trig_alias])
    for oc in range(FC):
        which = oc % 2
        stream_wcols(core, W["W_inT"], 1024, oc * 128, (oc + 1) * 128, which,
                     colsum=True, bias=W["b_in"])
        wb = core.wbf[which]
        cr = core.colr[which]
        cb = core.colb[which]
        for s in range(NS):
            pp = core.bank()
            sl = slice(s * 512, (s + 1) * 512)
            mm_chain(p, pp[:, 0:512],
                     [(wb[:, k * 128:(k + 1) * 128],
                       xt[:, k * TOK + s * 512:k * TOK + (s + 1) * 512]) for k in range(FC)],
                     reads=[wb, cr, cb, core.rows, core.onesrow, xt], writes=[pp],
                     extras=[(cr[0:1, 0:128], core.rows[0:1, sl]),
                             (cb[0:1, 0:128], core.onesrow[0:1, 0:512])])
            p.op("ve", lambda e, pp=pp: e.tensor_copy(out=stg[:, :], in_=pp[:, 0:512]),
                 reads=[pp], writes=[stg])
            emit_rope_block(core, stg, trig_alias, s * 512, 512, rotm,
                            qdm[:, oc * TOK + s * 512:oc * TOK + (s + 1) * 512], qdm, p)

    out_fm = xt
    emit_flash(core, qdm, kd, vd, acc, nml, out_fm, qoff,
               pieces=[(0, 0, 64)], stair=None, causal=True,
               kslab_bufs=(kslab, vslab), pstage=pstage, ptst=ptst,
               band=W["band"], bmask=bmask)

    for ob in range(2):
        which = ob % 2
        stream_wcols(core, W["W_outT"], 1024, ob * 512, (ob + 1) * 512, which,
                     colsum=False, bias=W["b_out"])
        wb = core.wbf[which]
        cb = core.colb[which]
        for i in range(NT):
            buf = core.xld[0]
            p.dma("sy", buf[:, :], x_src.ap()[i * 128:(i + 1) * 128, :],
                  reads=[x_src], writes=[buf])
            pp = core.bank()
            mm_chain(p, pp[:, 0:512],
                     [(out_fm[:, k * TOK + i * 128:k * TOK + (i + 1) * 128],
                       wb[:, k * 512:(k + 1) * 512]) for k in range(FC)],
                     reads=[out_fm, wb, cb, core.onesrow], writes=[pp],
                     extras=[(core.onesrow[0:1, 0:128], cb[0:1, 0:512])])
            p.op("ve", lambda e, pp=pp, buf=buf, ob=ob: e.tensor_tensor(
                out=core.dump[:, 0:512], in0=pp[:, 0:512],
                in1=buf[:, ob * 512:(ob + 1) * 512], op=ALU.add),
                reads=[pp, buf], writes=[core.wstage[0]])
            p.dma("gp", x_dst.ap()[i * 128:(i + 1) * 128, ob * 512:(ob + 1) * 512],
                  core.dump[:, 0:512], reads=[core.wstage[0]], writes=[x_dst],
                  is_output=True)


L3ATTN_W = dict(W_inT=[1024, 3072], b_in=[3072], W_outT=[1024, 1024],
                b_out=[1024], rotm=[128, 128], band=[128, 10112],
                qoffv=[1, 1])


def build_l3_attn_program(qoff=0):
    p = Prog("l3attn")
    xf = p.dram("xf", [SEQ, C], F32, "ExternalInput")
    x_in = p.dram("x_in", [TOK, C], F32, "ExternalInput")
    x_out = p.dram("x_out", [TOK, C], F32, "ExternalOutput")
    cident = p.dram("cident", [128, 128], F32, "ExternalInput")
    W = {k: p.dram(k, v, BF16 if k == "band" else F32, "ExternalInput")
         for k, v in L3ATTN_W.items()}
    W["_kd"] = p.dram("kd", [1024, SEQ], BF16)
    W["_vd"] = p.dram("vd", [SEQ, 1024], BF16)
    core = Core(p)
    emit_load_ident(core, cident)
    emit_l3_attn(core, W, x_in, x_out, xf, 0)
    return p.build()


def host_band(qoff):
    import ml_dtypes
    b = np.zeros((128, 10112), np.float32)
    for pp in range(128):
        b[pp, 1921 + qoff + pp:] = -1e30
    return b.astype(ml_dtypes.bfloat16)


def host_stair():
    m = np.zeros((4, 128, 512), np.float32)
    for v in range(4):
        for pp in range(128):
            m[v, pp, v * 128 + pp + 1:] = -1e30
    return m.reshape(128 * 4, 512).reshape(4, 128, 512).transpose(1, 0, 2).reshape(128, 2048)


def host_rotl3():
    R = np.zeros((128, 128), np.float32)
    for pp in range(128):
        q = pp + 32 if pp % 64 < 32 else pp - 32
        R[q, pp] = 1.0
    return R


def host_l3_attn_weights(inp):
    return dict(W_inT=np.ascontiguousarray(np.asarray(inp["L3_attn_W_in"], np.float32).T),
                b_in=np.asarray(inp["L3_attn_b_in"], np.float32),
                W_outT=np.ascontiguousarray(np.asarray(inp["L3_attn_W_out"], np.float32).T),
                b_out=np.asarray(inp["L3_attn_b_out"], np.float32),
                rotm=host_rotl3())


def emit_flash(core, qdm, kd_dram, vd_dram, acc, nml, out_fm, qoff,
               pieces, stair, causal=True, out_mask=None, kmax_all=SEQ,
               kslab_bufs=None, pstage=None, ptst=None, H=16, band=None,
               bmask=None):
    p = core.p
    ksl, vsl = kslab_bufs
    NKB = SEQ // 512
    kw = sum(w for (_, _, w) in pieces)
    p.op("ve", lambda e: e.memset(nml[:, 0:256], 3.0e38), writes=[nml])
    p.op("ve", lambda e: e.memset(nml[:, 256:512], 0.0), writes=[nml])
    for z in range(4):
        p.op("ve", lambda e, z=z: e.memset(acc.h[:, z * 4096:(z + 1) * 4096], 0.0),
             writes=[acc])
    for hh in range(2):
        for kb in range(NKB):
            if kb * 512 >= kmax_all:
                continue
            kslab = ksl
            nkch = 8 * kw // 128
            p.dma("sy", kslab.h[:, 0:nkch * 512].rearrange(
                "p (k c) -> p k c", k=nkch),
                kd_dram.ap()[hh * 8 * kw:(hh + 1) * 8 * kw,
                             kb * 512:(kb + 1) * 512].rearrange(
                "(k p) c -> p k c", p=128), reads=[kd_dram], writes=[kslab])
            p.dma("sy", vsl.h[:, 0:4 * 1024].rearrange("p (k c) -> p k c", k=4),
                  vd_dram.ap()[kb * 512:(kb + 1) * 512, :].rearrange(
                  "(k p) c -> p k c", p=128), reads=[vd_dram], writes=[vsl])
            if causal:
                for qi in range(NT):
                    bc0 = 1920 + 512 * kb - 128 * qi
                    p.dma("sy", bmask[:, qi * 512:(qi + 1) * 512],
                          band.ap()[:, bc0:bc0 + 512], reads=[band], writes=[bmask])
            for hl in range(8):
                h = hh * 8 + hl
                for qi in range(NT):
                    pp = core.bank()
                    pairs = []
                    for (qro, kro, w) in pieces:
                        qpart = (h % 2) * 64 + qro
                        lhs = qdm.h[qpart:qpart + w,
                                    (h // 2) * TOK + qi * 128:(h // 2) * TOK + (qi + 1) * 128]
                        krow = hl * kw + kro
                        kch, kof = divmod(krow, 128)
                        rhs = kslab.h[kof:kof + w, kch * 512:(kch + 1) * 512]
                        pairs.append((lhs, rhs))
                    mm_chain(p, pp[:, 0:512], pairs, reads=[qdm, kslab], writes=[pp])
                    nmc = nml.h[:, h * 16 + qi:h * 16 + qi + 1]
                    lc = nml.h[:, 256 + h * 16 + qi:256 + h * 16 + qi + 1]
                    sc_src = pp[:, 0:512]
                    if causal:
                        p.op("ve", lambda e, pp=pp, qi=qi: e.tensor_tensor(
                            out=core.dump[:, 0:512], in0=pp[:, 0:512],
                            in1=bmask[:, qi * 512:(qi + 1) * 512], op=ALU.add),
                            reads=[pp, bmask], writes=[core.wstage[0]])
                        sc_src = core.dump[:, 0:512]
                        sc_reads = [core.wstage[0]]
                    else:
                        sc_reads = [pp]
                    p.op("ve", lambda e, sc_src=sc_src: e.tensor_reduce(
                        out=core.diag[:, 4:5], in_=sc_src, axis=AX.X, op=ALU.max,
                        negate=True), reads=sc_reads, writes=[core.diag])
                    p.op("ve", lambda e, nmc=nmc: e.tensor_tensor(
                        out=core.diag[:, 5:6], in0=core.diag[:, 4:5], in1=nmc,
                        op=ALU.min), reads=[core.diag, nml], writes=[core.diag])
                    p.op("ve", lambda e, nmc=nmc: e.tensor_tensor(
                        out=core.diag[:, 6:7], in0=core.diag[:, 5:6], in1=nmc,
                        op=ALU.subtract), reads=[core.diag, nml], writes=[core.diag])
                    p.op("se", lambda e: e.activation(out=core.diag[:, 6:7],
                         in_=core.diag[:, 6:7], func=AF.Exp),
                         reads=[core.diag], writes=[core.diag])
                    p.op("ve", lambda e, nmc=nmc: e.tensor_copy(out=nmc,
                         in_=core.diag[:, 5:6]), reads=[core.diag], writes=[nml])
                    p.op("se", lambda e, sc_src=sc_src, nmc=nmc: e.activation(
                        out=pstage[:, 0:512], in_=sc_src, func=AF.Exp,
                        bias=nmc, accum_out=core.diag[:, 7:8]),
                        reads=sc_reads + [nml], writes=[pstage, core.diag])
                    p.op("ve", lambda e, lc=lc: e.scalar_tensor_tensor(
                        out=lc, in0=lc, scalar=core.diag[:, 6:7],
                        in1=core.diag[:, 7:8], op0=ALU.mult, op1=ALU.add),
                        reads=[core.diag, nml], writes=[nml])

                    def trf(e):
                        r = None
                        for b in range(4):
                            r = e.transpose(
                                core.ptr.h[:, :].bitcast(BF16)[:, b * 128:(b + 1) * 128],
                                pstage[:, b * 128:(b + 1) * 128], core.identb[:, :])
                        return r
                    p.op("pe", trf, reads=[pstage, core.identb], writes=[core.ptr])
                    p.op("ve", lambda e: e.tensor_copy(out=ptst[:, 0:512],
                         in_=core.ptr.h[:, :].bitcast(BF16)[:, 0:512]),
                         reads=[core.ptr], writes=[ptst])
                    pv = core.bank()
                    mm_chain(p, pv[:, 0:64],
                             [(ptst[:, b * 128:(b + 1) * 128],
                               vsl.h[:, b * 1024 + h * 64:b * 1024 + (h + 1) * 64])
                              for b in range(4)],
                             reads=[ptst, vsl], writes=[pv])
                    ac = acc.h[:, (h * 16 + qi) * 64:(h * 16 + qi + 1) * 64]
                    p.op("ve", lambda e, ac=ac, pv=pv: e.scalar_tensor_tensor(
                        out=ac, in0=ac, scalar=core.diag[:, 6:7], in1=pv[:, 0:64],
                        op0=ALU.mult, op1=ALU.add),
                        reads=[core.diag, pv, acc], writes=[acc])
    for h in range(H):
        for qi in range(NT):
            lc = nml.h[:, 256 + h * 16 + qi:256 + h * 16 + qi + 1]
            p.op("ve", lambda e, lc=lc: e.reciprocal(out=core.diag[:, 8:9], in_=lc),
                 reads=[nml], writes=[core.diag])
            if out_mask is not None:
                mb, stride = out_mask
                p.op("ve", lambda e, h=h, qi=qi, mb=mb, stride=stride: e.tensor_tensor(
                    out=core.diag[:, 8:9], in0=core.diag[:, 8:9],
                    in1=mb.h[:, qi * 16 + h:qi * 16 + h + 1], op=ALU.mult),
                    reads=[core.diag, mb], writes=[core.diag])
            ac = acc.h[:, (h * 16 + qi) * 64:(h * 16 + qi + 1) * 64]
            p.op("ve", lambda e, ac=ac: e.tensor_scalar(
                out=pstage[:, 0:64], in0=ac, scalar1=core.diag[:, 8:9],
                scalar2=None, op0=ALU.mult), reads=[acc, core.diag], writes=[pstage])
            p.op("pe", lambda e: e.transpose(
                core.ptr.h[:, :].bitcast(BF16)[0:64, 0:128],
                pstage[:, 0:64], core.identb[:, :]),
                reads=[pstage, core.identb], writes=[core.ptr])
            base = (h % 2) * 64
            c0 = (h // 2) * TOK + qi * 128
            p.op("ve", lambda e, base=base, c0=c0: e.tensor_copy(
                out=out_fm.h[base:base + 64, c0:c0 + 128],
                in_=core.ptr.h[:, :].bitcast(BF16)[0:64, 0:128]),
                reads=[core.ptr], writes=[out_fm])



def _l2_invth(p, core, it):
    emit_fmod(p, it[:, 0:1], it[:, 0:1], 16.0, it[:, 9:10],
              it.h[:, 9:10].bitcast(mybir.dt.int32), [it], [it], it)
    p.op("ve", lambda e: e.tensor_scalar(out=it[:, 0:1], in0=it[:, 0:1],
         scalar1=float(-np.log(10000.0) / 16.0), scalar2=None, op0=ALU.mult),
         reads=[it], writes=[it])
    p.op("se", lambda e: e.activation(out=it[:, 0:1], in_=it[:, 0:1], func=AF.Exp),
         reads=[it], writes=[it])


def emit_flash2(core, qdm_c, qdm_r, kd_c, kd_r, vd_dram, acc, nml, out_fm, qoff,
                stair, kslab, vslab, pstage, ptst, band=None, bmask=None):
    p = core.p
    NKB = SEQ // 512
    for hh in range(2):
        p.op("ve", lambda e: e.memset(nml[:, 0:256], 3.0e38), writes=[nml])
        p.op("ve", lambda e: e.memset(nml[:, 256:512], 0.0), writes=[nml])
        for z in range(2):
            p.op("ve", lambda e, z=z: e.memset(acc.h[:, z * 4096:(z + 1) * 4096], 0.0),
                 writes=[acc])
        for kb in range(NKB):
            p.dma("sy", kslab.h[:, 0:4 * 512].rearrange("p (k c) -> p k c", k=4),
                  kd_c.ap()[hh * 512:(hh + 1) * 512,
                            kb * 512:(kb + 1) * 512].rearrange(
                  "(k p) c -> p k c", p=128), reads=[kd_c], writes=[kslab])
            p.dma("sy", kslab.h[:, 4 * 512:8 * 512].rearrange("p (k c) -> p k c", k=4),
                  kd_r.ap()[hh * 512:(hh + 1) * 512,
                            kb * 512:(kb + 1) * 512].rearrange(
                  "(k p) c -> p k c", p=128), reads=[kd_r], writes=[kslab])
            p.dma("sy", vslab.h[:, 0:4 * 1024].rearrange("p (k c) -> p k c", k=4),
                  vd_dram.ap()[kb * 512:(kb + 1) * 512, :].rearrange(
                  "(k p) c -> p k c", p=128), reads=[vd_dram], writes=[vslab])
            for qi in range(NT):
                bc0 = 1920 + 512 * kb - 128 * qi
                p.dma("sy", bmask[:, qi * 512:(qi + 1) * 512],
                      band.ap()[:, bc0:bc0 + 512], reads=[band], writes=[bmask])
            for hl in range(8):
                h = hh * 8 + hl
                kch = hl // 2
                kof = (hl % 2) * 64
                for qi in range(NT):
                    pp = core.bank()
                    qc0 = (h // 2) * TOK + qi * 128
                    qb = (h % 2) * 64
                    pairs = [
                        (qdm_c.h[qb:qb + 64, qc0:qc0 + 128],
                         kslab.h[kof:kof + 64, kch * 512:(kch + 1) * 512]),
                        (qdm_r.h[qb:qb + 32, qc0:qc0 + 128],
                         kslab.h[kof:kof + 32, (4 + kch) * 512:(5 + kch) * 512]),
                    ]
                    mm_chain(p, pp[:, 0:512], pairs, reads=[qdm_c, qdm_r, kslab],
                             writes=[pp])
                    nmc = nml.h[:, hl * 16 + qi:hl * 16 + qi + 1]
                    lc = nml.h[:, 256 + hl * 16 + qi:256 + hl * 16 + qi + 1]
                    p.op("ve", lambda e, pp=pp, qi=qi: e.tensor_tensor(
                        out=core.dump[:, 0:512], in0=pp[:, 0:512],
                        in1=bmask[:, qi * 512:(qi + 1) * 512], op=ALU.add),
                        reads=[pp, bmask], writes=[core.wstage[0]])
                    sc_src = core.dump[:, 0:512]
                    sc_reads = [core.wstage[0]]
                    p.op("ve", lambda e, sc_src=sc_src: e.tensor_reduce(
                        out=core.diag[:, 4:5], in_=sc_src, axis=AX.X, op=ALU.max,
                        negate=True), reads=sc_reads, writes=[core.diag])
                    p.op("ve", lambda e, nmc=nmc: e.tensor_tensor(
                        out=core.diag[:, 5:6], in0=core.diag[:, 4:5], in1=nmc,
                        op=ALU.min), reads=[core.diag, nml], writes=[core.diag])
                    p.op("ve", lambda e, nmc=nmc: e.tensor_tensor(
                        out=core.diag[:, 6:7], in0=core.diag[:, 5:6], in1=nmc,
                        op=ALU.subtract), reads=[core.diag, nml], writes=[core.diag])
                    p.op("se", lambda e: e.activation(out=core.diag[:, 6:7],
                         in_=core.diag[:, 6:7], func=AF.Exp),
                         reads=[core.diag], writes=[core.diag])
                    p.op("ve", lambda e, nmc=nmc: e.tensor_copy(out=nmc,
                         in_=core.diag[:, 5:6]), reads=[core.diag], writes=[nml])
                    p.op("se", lambda e, sc_src=sc_src, nmc=nmc: e.activation(
                        out=pstage[:, 0:512], in_=sc_src, func=AF.Exp,
                        bias=nmc, accum_out=core.diag[:, 7:8]),
                        reads=sc_reads + [nml], writes=[pstage, core.diag])
                    p.op("ve", lambda e, lc=lc: e.scalar_tensor_tensor(
                        out=lc, in0=lc, scalar=core.diag[:, 6:7],
                        in1=core.diag[:, 7:8], op0=ALU.mult, op1=ALU.add),
                        reads=[core.diag, nml], writes=[nml])

                    def trf(e):
                        r = None
                        for b in range(4):
                            r = e.transpose(
                                core.ptr.h[:, :].bitcast(BF16)[:, b * 128:(b + 1) * 128],
                                pstage[:, b * 128:(b + 1) * 128], core.identb[:, :])
                        return r
                    p.op("pe", trf, reads=[pstage, core.identb], writes=[core.ptr])
                    p.op("ve", lambda e: e.tensor_copy(out=ptst[:, 0:512],
                         in_=core.ptr.h[:, :].bitcast(BF16)[:, 0:512]),
                         reads=[core.ptr], writes=[ptst])
                    pv = core.bank()
                    mm_chain(p, pv[:, 0:64],
                             [(ptst[:, b * 128:(b + 1) * 128],
                               vslab.h[:, b * 1024 + h * 64:b * 1024 + (h + 1) * 64])
                              for b in range(4)],
                             reads=[ptst, vslab], writes=[pv])
                    ac = acc.h[:, (hl * 16 + qi) * 64:(hl * 16 + qi + 1) * 64]
                    p.op("ve", lambda e, ac=ac, pv=pv: e.scalar_tensor_tensor(
                        out=ac, in0=ac, scalar=core.diag[:, 6:7], in1=pv[:, 0:64],
                        op0=ALU.mult, op1=ALU.add),
                        reads=[core.diag, pv, acc], writes=[acc])
        for hl in range(8):
            h = hh * 8 + hl
            for qi in range(NT):
                lc = nml.h[:, 256 + hl * 16 + qi:256 + hl * 16 + qi + 1]
                p.op("ve", lambda e, lc=lc: e.reciprocal(out=core.diag[:, 8:9], in_=lc),
                     reads=[nml], writes=[core.diag])
                ac = acc.h[:, (hl * 16 + qi) * 64:(hl * 16 + qi + 1) * 64]
                p.op("ve", lambda e, ac=ac: e.tensor_scalar(
                    out=pstage[:, 0:64], in0=ac, scalar1=core.diag[:, 8:9],
                    scalar2=None, op0=ALU.mult), reads=[acc, core.diag], writes=[pstage])
                p.op("pe", lambda e: e.transpose(
                    core.ptr.h[:, :].bitcast(BF16)[0:64, 0:128],
                    pstage[:, 0:64], core.identb[:, :]),
                    reads=[pstage, core.identb], writes=[core.ptr])
                base = (h % 2) * 64
                c0 = (h // 2) * TOK + qi * 128
                p.op("ve", lambda e, base=base, c0=c0: e.tensor_copy(
                    out=out_fm.h[base:base + 64, c0:c0 + 128],
                    in_=core.ptr.h[:, :].bitcast(BF16)[0:64, 0:128]),
                    reads=[core.ptr], writes=[out_fm])


def emit_l2_attn(core, W, x_src, x_dst, xf, qoff):
    p = core.p
    xt = p.sbuf("l2a_xt", [128, FC * TOK], BF16)
    qdm = p.sbuf("l2a_q", [128, FC * TOK], BF16)
    qdmr = p.sbuf("l2a_qr", [128, FC * TOK], BF16)
    acc = p.sbuf("l2a_acc", [128, 12288], F32)
    trig = Buf(acc.h, "l2a_trig"); p.bufs.append(trig)
    trigscr = Buf(acc.h, "l2a_tscr"); p.bufs.append(trigscr)
    kvn = p.sbuf("l2a_kvn", [128, 2 * TOK], BF16)
    qcn = KSlabSlice(acc, 16384, base_ap=acc.h[:, :].bitcast(BF16))
    p.bufs.append(qcn)
    kslab = Buf(xt.h, "l2a_ks"); p.bufs.append(kslab)
    vslab = Buf(xt.h, "l2a_vs"); p.bufs.append(vslab)
    stairb = Buf(xt.h, "l2a_stair"); p.bufs.append(stairb)
    pstage = p.sbuf("l2a_ps", [128, 512], BF16)
    ptst = p.sbuf("l2a_pt", [128, 512], BF16)
    nml = p.sbuf("l2a_nml", [128, 512], F32)
    rotm = p.sbuf("l2a_rot", [128, 128], BF16)
    stg = p.sbuf("l2a_stg", [128, 512], BF16)
    gkv = p.sbuf("l2a_gkv", [128, 2], F32)
    eps6 = p.sbuf("l2a_eps", [128, 1], F32)
    kvld = p.sbuf("l2a_kvld", [128, 256], F32)
    bmask = KSlabSlice(xt, 8192)
    p.bufs.append(bmask)
    kd_c, kd_r, vd = W["_kdc"], W["_kdr"], W["_vd"]
    SC96 = float(1.0 / np.sqrt(96.0))
    qoffcol = emit_qoffcol(core, W)

    p.op("ve", lambda e: e.memset(eps6[:, :], 1e-6), writes=[eps6])
    load_fmcol(core, gkv, W["g_kv"], 256, 0)
    p.dma("sy", core.dump[:, 0:128], W["rotm2"].ap()[:, :], reads=[W["rotm2"]],
          writes=[core.wstage[0]])
    p.op("ve", lambda e: e.tensor_copy(out=rotm[:, :], in_=core.dump[:, 0:128]),
         reads=[core.wstage[0]], writes=[rotm])

    def kv_side(slab):
        emit_xt_rows(core, xf, xt, slab)
        emit_trig(core, trig, trigscr, slab * 2048, _l2_invth, sgn_period=None,
                  scr_off=6144)
        which = 0
        stream_wcols(core, W["W_kv"], 1024, 0, 256, 0, colsum=True)
        wb = core.wbf[0]
        cr = core.colr[0]
        for i in range(NT):
            pp = core.bank()
            mm_chain(p, pp[:, 0:256],
                     [(xt[:, k * TOK + i * 128:k * TOK + (i + 1) * 128],
                       wb[:, k * 256:(k + 1) * 256]) for k in range(FC)],
                     reads=[wb, cr, core.rows, xt], writes=[pp],
                     extras=[(core.rows[0:1, i * 128:(i + 1) * 128], cr[0:1, 0:256])])
            p.op("se", lambda e, pp=pp: e.activation(out=kvld[:, :], in_=pp[:, 0:256],
                 func=AF.Copy, accum_out=None), reads=[pp], writes=[kvld])
            p.op("se", lambda e: e.activation(out=core.dump[:, 0:256], in_=kvld[:, :],
                 func=AF.Square, accum_out=core.diag[:, 10:11]),
                 reads=[kvld], writes=[core.wstage[0], core.diag])
            p.op("ve", lambda e: e.tensor_scalar(out=core.diag[:, 10:11],
                 in0=core.diag[:, 10:11], scalar1=1.0 / 256.0, scalar2=None,
                 op0=ALU.mult), reads=[core.diag], writes=[core.diag])
            p.op("se", lambda e: e.activation(out=core.diag[:, 10:11],
                 in_=core.diag[:, 10:11], func=AF.Ln, bias=eps6[:, 0:1]),
                 reads=[core.diag, eps6], writes=[core.diag])
            p.op("se", lambda e: e.activation(out=core.diag[:, 10:11],
                 in_=core.diag[:, 10:11], func=AF.Exp, scale=-0.5),
                 reads=[core.diag], writes=[core.diag])
            p.op("ve", lambda e, i=i: e.tensor_scalar(out=core.diag[:, 0:128],
                 in0=core.ident[:, :], scalar1=core.diag[:, 10:11], scalar2=None,
                 op0=ALU.mult), reads=[core.ident, core.diag], writes=[core.diag])
            for b in range(2):
                p.op("pe", lambda e, b=b: e.matmul(core.ptr[:, b * 128:(b + 1) * 128],
                     kvld[:, b * 128:(b + 1) * 128], core.diag[:, 0:128],
                     start=True, stop=True), reads=[kvld, core.diag], writes=[core.ptr])
                p.op("ve", lambda e, b=b, i=i: e.tensor_scalar(
                    out=kvn[:, b * TOK + i * 128:b * TOK + (i + 1) * 128],
                    in0=core.ptr[:, b * 128:(b + 1) * 128],
                    scalar1=gkv[:, b:b + 1], scalar2=None, op0=ALU.mult),
                    reads=[core.ptr, gkv], writes=[kvn])
        for oc in range(FC):
            stream_wcols(core, W["W_kdec"], 256, oc * 128, (oc + 1) * 128, oc % 2,
                         colsum=False)
            wb = core.wbf[oc % 2]
            for s in range(NS):
                pp = core.bank()
                mm_chain(p, pp[:, 0:512],
                         [(wb[:, k * 128:(k + 1) * 128],
                           kvn[:, k * TOK + s * 512:k * TOK + (s + 1) * 512]) for k in range(2)],
                         reads=[wb, kvn], writes=[pp])
                p.op("ve", lambda e, pp=pp: e.tensor_copy(out=stg[:, :],
                     in_=pp[:, 0:512]), reads=[pp], writes=[stg])
                p.dma("gp", kd_c.ap()[oc * 128:(oc + 1) * 128,
                                      slab * 2048 + s * 512:slab * 2048 + (s + 1) * 512],
                      stg[:, :], reads=[stg], writes=[kd_c])
        for ob in range(2):
            stream_wcols(core, W["W_vdec"], 256, ob * 512, (ob + 1) * 512, ob % 2,
                         colsum=False)
            wb = core.wbf[ob % 2]
            for i in range(NT):
                pp = core.bank()
                mm_chain(p, pp[:, 0:512],
                         [(kvn[:, k * TOK + i * 128:k * TOK + (i + 1) * 128],
                           wb[:, k * 512:(k + 1) * 512]) for k in range(2)],
                         reads=[wb, kvn], writes=[pp])
                p.op("ve", lambda e, pp=pp: e.tensor_copy(out=stg[:, :],
                     in_=pp[:, 0:512]), reads=[pp], writes=[stg])
                p.dma("gp", vd.ap()[slab * 2048 + i * 128:slab * 2048 + (i + 1) * 128,
                                    ob * 512:(ob + 1) * 512],
                      stg[:, :], reads=[stg], writes=[vd])
        for oc in range(FC):
            stream_wcols(core, W["W_kropeP"], 1024, oc * 128, (oc + 1) * 128, oc % 2,
                         colsum=True)
            wb = core.wbf[oc % 2]
            cr = core.colr[oc % 2]
            for s in range(NS):
                pp = core.bank()
                sl = slice(s * 512, (s + 1) * 512)
                mm_chain(p, pp[:, 0:512],
                         [(wb[:, k * 128:(k + 1) * 128],
                           xt[:, k * TOK + s * 512:k * TOK + (s + 1) * 512]) for k in range(FC)],
                         reads=[wb, cr, core.rows, xt], writes=[pp],
                         extras=[(cr[0:1, 0:128], core.rows[0:1, sl])])
                p.op("ve", lambda e, pp=pp: e.tensor_copy(out=stg[:, :],
                     in_=pp[:, 0:512]), reads=[pp], writes=[stg])
                emit_rope_block(core, stg, trig, s * 512, 512, rotm,
                                stg[:, 0:512], stg, p)
                p.dma("gp", kd_r.ap()[oc * 128:(oc + 1) * 128,
                                      slab * 2048 + s * 512:slab * 2048 + (s + 1) * 512],
                      stg[:, :], reads=[stg], writes=[kd_r])

    for slab in range(4):
        kv_side(slab)

    emit_xt_rows(core, x_src, xt, 0, row0=0)
    emit_trig(core, trig, trigscr, 0, _l2_invth, sgn_period=None, scr_off=6144,
              pos_col=qoffcol[:, 0:1], pos_col_buf=qoffcol)
    p.barrier()
    for oc in range(4):
        stream_wcols(core, W["W_q"], 1024, oc * 128, (oc + 1) * 128, oc % 2,
                     colsum=True)
        wb = core.wbf[oc % 2]
        cr = core.colr[oc % 2]
        for s in range(NS):
            pp = core.bank()
            sl = slice(s * 512, (s + 1) * 512)
            mm_chain(p, pp[:, 0:512],
                     [(wb[:, k * 128:(k + 1) * 128],
                       xt[:, k * TOK + s * 512:k * TOK + (s + 1) * 512]) for k in range(FC)],
                     reads=[wb, cr, core.rows, xt], writes=[pp],
                     extras=[(cr[0:1, 0:128], core.rows[0:1, sl])])
            p.op("ve", lambda e, oc=oc, s=s, pp=pp: e.tensor_copy(
                out=qcn[:, oc * TOK + s * 512:oc * TOK + (s + 1) * 512],
                in_=pp[:, 0:512]), reads=[pp], writes=[qcn])
    for oc in range(FC):
        stream_wcols(core, W["W_qdec"], 512, oc * 128, (oc + 1) * 128, oc % 2,
                     colsum=False)
        wb = core.wbf[oc % 2]
        for s in range(NS):
            pp = core.bank()
            mm_chain(p, pp[:, 0:512],
                     [(wb[:, k * 128:(k + 1) * 128],
                       qcn[:, k * TOK + s * 512:k * TOK + (s + 1) * 512]) for k in range(4)],
                     reads=[wb, qcn], writes=[pp])
            p.op("ve", lambda e, oc=oc, s=s, pp=pp: e.tensor_scalar(
                out=qdm[:, oc * TOK + s * 512:oc * TOK + (s + 1) * 512],
                in0=pp[:, 0:512], scalar1=SC96, scalar2=None, op0=ALU.mult),
                reads=[pp], writes=[qdm])
    for halfn in range(2):
        p.op("ve", lambda e, halfn=halfn: e.tensor_scalar(
            out=trig[:, halfn * 2048:(halfn + 1) * 2048],
            in0=trig[:, halfn * 2048:(halfn + 1) * 2048],
            scalar1=SC96, scalar2=None, op0=ALU.mult), reads=[trig], writes=[trig])
    for oc in range(FC):
        stream_wcols(core, W["W_qropeP"], 512, oc * 128, (oc + 1) * 128, oc % 2,
                     colsum=False)
        wb = core.wbf[oc % 2]
        for s in range(NS):
            pp = core.bank()
            mm_chain(p, pp[:, 0:512],
                     [(wb[:, k * 128:(k + 1) * 128],
                       qcn[:, k * TOK + s * 512:k * TOK + (s + 1) * 512]) for k in range(4)],
                     reads=[wb, qcn], writes=[pp])
            p.op("ve", lambda e, pp=pp: e.tensor_copy(out=stg[:, :], in_=pp[:, 0:512]),
                 reads=[pp], writes=[stg])
            emit_rope_block(core, stg, trig, s * 512, 512, rotm, stg[:, 0:512], stg, p)
            p.op("ve", lambda e, oc=oc, s=s: e.tensor_copy(
                out=qdmr[:, oc * TOK + s * 512:oc * TOK + (s + 1) * 512],
                in_=stg[:, 0:512]), reads=[stg], writes=[qdmr])

    p.barrier()
    out_fm = Buf(qdm.h, "l2a_out"); p.bufs.append(out_fm)
    emit_l2_flash_wrap(core, qdm, qdmr, kd_c, kd_r, vd, acc, nml, out_fm, qoff,
                       xt, pstage, ptst, W["band"], bmask)

    for ob in range(2):
        which = ob % 2
        stream_wcols(core, W["W_o"], 1024, ob * 512, (ob + 1) * 512, which,
                     colsum=False)
        wb = core.wbf[which]
        for i in range(NT):
            buf = core.xld[0]
            p.dma("sy", buf[:, :], x_src.ap()[i * 128:(i + 1) * 128, :],
                  reads=[x_src], writes=[buf])
            pp = core.bank()
            mm_chain(p, pp[:, 0:512],
                     [(out_fm.h[(k % 2) * 64:(k % 2) * 64 + 64,
                                (k // 2) * TOK + i * 128:(k // 2) * TOK + (i + 1) * 128]
                       if False else
                       out_fm.h[(k % 2) * 64:(k % 2 + 1) * 64,
                                (k // 2) * TOK + i * 128:(k // 2) * TOK + (i + 1) * 128],
                       wb[:, (0) * 512:512]) for k in range(0)] or
                     [(fm_head_block(out_fm, k, i),
                       wb[:, k * 512:(k + 1) * 512]) for k in range(FC)],
                     reads=[out_fm, wb], writes=[pp])
            p.op("ve", lambda e, pp=pp, buf=buf, ob=ob: e.tensor_tensor(
                out=core.dump[:, 0:512], in0=pp[:, 0:512],
                in1=buf[:, ob * 512:(ob + 1) * 512], op=ALU.add),
                reads=[pp, buf], writes=[core.wstage[0]])
            p.dma("gp", x_dst.ap()[i * 128:(i + 1) * 128, ob * 512:(ob + 1) * 512],
                  core.dump[:, 0:512], reads=[core.wstage[0]], writes=[x_dst],
                  is_output=True)


def fm_head_block(out_fm, k, i):
    return out_fm.h[:, k * TOK + i * 128:k * TOK + (i + 1) * 128]


def emit_l2_flash_wrap(core, qdm_c, qdm_r, kd_c, kd_r, vd, acc, nml, out_fm,
                       qoff, xtbuf, pstage, ptst, band, bmask):
    emit_flash2(core, qdm_c, qdm_r, kd_c, kd_r, vd, acc, nml, out_fm, qoff,
                None, KSlabSlice(xtbuf, 0), KSlabSlice(xtbuf, 4096),
                pstage, ptst, band=band, bmask=bmask)


class KSlabSlice:
    def __init__(self, buf, off, base_ap=None):
        self.buf = buf
        self.off = off
        self.h = _APShift(base_ap if base_ap is not None else buf.h, off)

    def __getitem__(self, idx):
        return self.h[idx]

    @property
    def name(self):
        return self.buf.name

    @property
    def last_w(self):
        return self.buf.last_w

    @last_w.setter
    def last_w(self, v):
        self.buf.last_w = v

    @property
    def readers(self):
        return self.buf.readers

    @readers.setter
    def readers(self, v):
        self.buf.readers = v

    @property
    def pend_w(self):
        return self.buf.pend_w

    @pend_w.setter
    def pend_w(self, v):
        self.buf.pend_w = v


class _APShift:
    def __init__(self, h, off):
        self._h = h
        self._off = off

    def __getitem__(self, idx):
        pslice, cslice = idx
        return self._h[pslice, slice(self._off + (cslice.start or 0),
                                     self._off + cslice.stop)]


class StairSlice:
    def __init__(self, buf):
        self.buf = buf

    def __getitem__(self, idx):
        pslice, cslice = idx
        f = self.buf.h[:, :].bitcast(F32)
        return f[pslice, slice(4096 + (cslice.start or 0), 4096 + cslice.stop)]

    @property
    def name(self):
        return self.buf.name

    @property
    def last_w(self):
        return self.buf.last_w

    @last_w.setter
    def last_w(self, v):
        self.buf.last_w = v

    @property
    def readers(self):
        return self.buf.readers

    @readers.setter
    def readers(self, v):
        self.buf.readers = v

    @property
    def pend_w(self):
        return self.buf.pend_w

    @pend_w.setter
    def pend_w(self, v):
        self.buf.pend_w = v


L2ATTN_W = dict(W_kv=[1024, 256], g_kv=[256], W_kdec=[256, 1024],
                W_vdec=[256, 1024], W_q=[1024, 512], W_qdec=[512, 1024],
                W_kropeP=[1024, 1024], W_qropeP=[512, 1024], W_o=[1024, 1024],
                rotm2=[128, 128], band=[128, 10112], qoffv=[1, 1])


def build_l2_attn_program(qoff=0):
    p = Prog("l2attn")
    xf = p.dram("xf", [SEQ, C], F32, "ExternalInput")
    x_in = p.dram("x_in", [TOK, C], F32, "ExternalInput")
    x_out = p.dram("x_out", [TOK, C], F32, "ExternalOutput")
    cident = p.dram("cident", [128, 128], F32, "ExternalInput")
    W = {k: p.dram(k, v, BF16 if k == "band" else F32, "ExternalInput")
         for k, v in L2ATTN_W.items()}
    W["_kdc"] = p.dram("kdc", [1024, SEQ], BF16)
    W["_kdr"] = p.dram("kdr", [1024, SEQ], BF16)
    W["_vd"] = p.dram("vd", [SEQ, 1024], BF16)
    core = Core(p)
    emit_load_ident(core, cident)
    emit_l2_attn(core, W, x_in, x_out, xf, 0)
    return p.build()


def host_rotl2():
    R = np.zeros((128, 128), np.float32)
    for pp in range(128):
        if pp % 32 < 16:
            R[pp + 16, pp] = -1.0
        else:
            R[pp - 16, pp] = 1.0
    return R


def host_l2_attn_weights(inp):
    A = lambda k: np.asarray(inp[k], np.float32)
    return dict(W_kv=A("L2_attn_W_kv"), g_kv=A("L2_attn_g_kv"),
                W_kdec=A("L2_attn_W_kdec"), W_vdec=A("L2_attn_W_vdec"),
                W_q=A("L2_attn_W_q"), W_qdec=A("L2_attn_W_qdec"),
                W_kropeP=_pad_rope_cols(A("L2_attn_W_krope")),
                W_qropeP=_pad_rope_cols(A("L2_attn_W_qrope")),
                W_o=A("L2_attn_W_o"), rotm2=host_rotl2())


def _pad_rope_cols(w):
    F = w.shape[0]
    out = np.zeros((F, 1024), np.float32)
    for h in range(16):
        out[:, h * 64:h * 64 + 32] = w[:, h * 32:(h + 1) * 32]
    return out



TOP_U = SEQ // 5


def emit_l0_attn(core, W, x_src, x_dst, xf, qoff):
    import os as _os
    _dbg = _os.environ.get("L0ADBG", "")
    p = core.p
    xt = p.sbuf("l0a_xt", [128, FC * TOK], BF16)
    qdm = p.sbuf("l0a_q", [128, FC * TOK], BF16)
    acc = p.sbuf("l0a_acc", [128, 16384], F32)
    kslab = Buf(xt.h, "l0a_ks"); p.bufs.append(kslab)
    vslab = Buf(xt.h, "l0a_vs"); p.bufs.append(vslab)
    pstage = p.sbuf("l0a_ps", [128, 512], BF16)
    ptst = p.sbuf("l0a_pt", [128, 512], BF16)
    nml = p.sbuf("l0a_nml", [128, 512], F32)
    stg = p.sbuf("l0a_stg", [128, 2048], F32)
    nsb = p.sbuf("l0a_nsb", [128, 1024], F32)
    bis = p.sbuf("l0a_bis", [128, 8], F32)
    g16 = p.sbuf("l0a_g16", [128, 16], F32)
    g16t = p.sbuf("l0a_g16t", [16, 128], F32)
    mcols = p.sbuf("l0a_mc", [128, 256], F32)
    e2 = p.sbuf("l0a_e2", [128, 2], BF16)
    msb = p.sbuf("l0a_msb", [16, 2048], F32)
    kd, vd, nd, md = W["_kd"], W["_vd"], W["_nd"], W["_md"]

    p.dma("sy", g16[:, :], W["G16"].ap()[:, :], reads=[W["G16"]], writes=[g16])
    p.dma("sy", g16t[:, :], W["G16T"].ap()[:, :], reads=[W["G16T"]], writes=[g16t])
    p.op("ve", lambda e: e.memset(e2[:, 0:2], 0.0), writes=[e2])
    p.op("ve", lambda e: e.memset(e2[0:64, 0:1], 1.0), writes=[e2])
    p.op("ve", lambda e: e.memset(e2[64:128, 1:2], 1.0), writes=[e2])

    for slab in range(4):
        emit_xt_rows(core, xf, xt, slab)
        for oc in range(FC):
            stream_wcols(core, W["WkT"], 1024, oc * 128, (oc + 1) * 128, oc % 2,
                         colsum=True, bias=W["bk"])
            wb = core.wbf[oc % 2]
            cr = core.colr[oc % 2]
            cb = core.colb[oc % 2]
            for s in range(NS):
                pp = core.bank()
                sl = slice(s * 512, (s + 1) * 512)
                mm_chain(p, pp[:, 0:512],
                         [(wb[:, k * 128:(k + 1) * 128],
                           xt[:, k * TOK + s * 512:k * TOK + (s + 1) * 512]) for k in range(FC)],
                         reads=[wb, cr, cb, core.rows, core.onesrow, xt], writes=[pp],
                         extras=[(cr[0:1, 0:128], core.rows[0:1, sl]),
                                 (cb[0:1, 0:128], core.onesrow[0:1, 0:512])])
                p.op("ve", lambda e, pp=pp: e.tensor_copy(out=pstage[:, :],
                     in_=pp[:, 0:512]), reads=[pp], writes=[pstage])
                p.dma("gp", kd.ap()[oc * 128:(oc + 1) * 128,
                                    slab * 2048 + s * 512:slab * 2048 + (s + 1) * 512],
                      pstage[:, :], reads=[pstage], writes=[kd])
        for ob in range(2):
            stream_wcols(core, W["WvT"], 1024, ob * 512, (ob + 1) * 512, ob % 2,
                         colsum=True, bias=W["bv"])
            wb = core.wbf[ob % 2]
            cr = core.colr[ob % 2]
            cb = core.colb[ob % 2]
            for i in range(NT):
                pp = core.bank()
                mm_chain(p, pp[:, 0:512],
                         [(xt[:, k * TOK + i * 128:k * TOK + (i + 1) * 128],
                           wb[:, k * 512:(k + 1) * 512]) for k in range(FC)],
                         reads=[wb, cr, cb, core.rows, core.onesrow, xt], writes=[pp],
                         extras=[(core.rows[0:1, i * 128:(i + 1) * 128], cr[0:1, 0:512]),
                                 (core.onesrow[0:1, 0:128], cb[0:1, 0:512])])
                p.op("ve", lambda e, pp=pp: e.tensor_copy(out=pstage[:, :],
                     in_=pp[:, 0:512]), reads=[pp], writes=[pstage])
                p.dma("gp", vd.ap()[slab * 2048 + i * 128:slab * 2048 + (i + 1) * 128,
                                    ob * 512:(ob + 1) * 512],
                      pstage[:, :], reads=[pstage], writes=[vd])
        for oc in range(FC):
            stream_wcols(core, W["WqT"], 1024, oc * 128, (oc + 1) * 128, oc % 2,
                         colsum=True, bias=W["bq"])
            wb = core.wbf[oc % 2]
            cr = core.colr[oc % 2]
            cb = core.colb[oc % 2]
            pn = core.pmisc
            for s in range(NS):
                pp = core.bank()
                sl = slice(s * 512, (s + 1) * 512)
                mm_chain(p, pp[:, 0:512],
                         [(wb[:, k * 128:(k + 1) * 128],
                           xt[:, k * TOK + s * 512:k * TOK + (s + 1) * 512]) for k in range(FC)],
                         reads=[wb, cr, cb, core.rows, core.onesrow, xt], writes=[pp],
                         extras=[(cr[0:1, 0:128], core.rows[0:1, sl]),
                                 (cb[0:1, 0:128], core.onesrow[0:1, 0:512])])
                p.op("se", lambda e, s=s, pp=pp: e.activation(
                    out=stg[:, s * 512:(s + 1) * 512], in_=pp[:, 0:512],
                    func=AF.Square), reads=[pp], writes=[stg])
            for s4 in range(4):
                p.op("pe", lambda e, s4=s4: e.matmul(pn[0:2, 0:512],
                     core.e2f[:, 0:2], stg[:, s4 * 512:(s4 + 1) * 512],
                     start=True, stop=True), reads=[stg, core.e2f], writes=[pn])
                p.op("ve", lambda e, s4=s4: e.tensor_copy(
                    out=msb[0:2, s4 * 512:(s4 + 1) * 512], in_=pn[0:2, 0:512]),
                    reads=[pn], writes=[msb])
            p.dma("gp", nd.ap()[2 * oc:2 * oc + 2, slab * 2048:(slab + 1) * 2048],
                  msb[0:2, :], reads=[msb], writes=[nd])

    p.dma("sy", nsb[:, :], nd.ap()[:, :].rearrange("h (tb t) -> (h tb) t", t=1024),
          reads=[nd], writes=[nsb])
    p.op("ve", lambda e: e.memset(bis[:, 0:1], 0.0), writes=[bis])
    p.op("ve", lambda e: e.memset(bis[:, 1:2], 1.0e6), writes=[bis])
    for it in range(44):
        p.op("ve", lambda e: e.tensor_tensor(out=bis[:, 2:3], in0=bis[:, 0:1],
             in1=bis[:, 1:2], op=ALU.add), reads=[bis], writes=[bis])
        p.op("ve", lambda e: e.tensor_scalar(out=bis[:, 2:3], in0=bis[:, 2:3],
             scalar1=0.5, scalar2=None, op0=ALU.mult), reads=[bis], writes=[bis])
        p.op("ve", lambda e: e.tensor_scalar(out=core.dump[:, 0:1024], in0=nsb[:, :],
             scalar1=bis[:, 2:3], scalar2=0.0, op0=ALU.is_gt, op1=ALU.add,
             accum_out=bis[:, 3:4]), reads=[nsb, bis], writes=[core.wstage[0], bis])
        p.op("pe", lambda e: e.matmul(core.pmisc[0:16, 0:1], g16[:, :], bis[:, 3:4],
             start=True, stop=True), reads=[g16, bis], writes=[core.pmisc])
        p.op("ve", lambda e: e.tensor_copy(out=bis[:, 6:7], in_=core.pmisc[0:16, 0:1])
             if False else e.tensor_copy(out=core.tmp32[0:16, 0:1], in_=core.pmisc[0:16, 0:1]),
             reads=[core.pmisc], writes=[core.tmp32])
        p.op("pe", lambda e: e.matmul(core.pmisc[:, 4:5], g16t[:, :], core.tmp32[0:16, 0:1],
             start=True, stop=True), reads=[g16t, core.tmp32], writes=[core.pmisc])
        p.op("ve", lambda e: e.tensor_copy(out=bis[:, 4:5], in_=core.pmisc[:, 4:5]),
             reads=[core.pmisc], writes=[bis])
        p.op("ve", lambda e: e.tensor_scalar(out=bis[:, 5:6], in0=bis[:, 4:5],
             scalar1=float(TOP_U), scalar2=None, op0=ALU.is_ge), reads=[bis], writes=[bis])
        p.op("ve", lambda e: e.tensor_tensor(out=bis[:, 6:7], in0=bis[:, 2:3],
             in1=bis[:, 0:1], op=ALU.subtract), reads=[bis], writes=[bis])
        p.op("ve", lambda e: e.scalar_tensor_tensor(out=bis[:, 0:1], in0=bis[:, 6:7],
             scalar=bis[:, 5:6], in1=bis[:, 0:1], op0=ALU.mult, op1=ALU.add),
             reads=[bis], writes=[bis])
        p.op("ve", lambda e: e.tensor_tensor(out=bis[:, 6:7], in0=bis[:, 2:3],
             in1=bis[:, 1:2], op=ALU.subtract), reads=[bis], writes=[bis])
        p.op("ve", lambda e: e.tensor_scalar(out=bis[:, 7:8], in0=bis[:, 5:6],
             scalar1=-1.0, scalar2=1.0, op0=ALU.mult, op1=ALU.add), reads=[bis], writes=[bis])
        p.op("ve", lambda e: e.scalar_tensor_tensor(out=bis[:, 1:2], in0=bis[:, 6:7],
             scalar=bis[:, 7:8], in1=bis[:, 1:2], op0=ALU.mult, op1=ALU.add),
             reads=[bis], writes=[bis])
    import os as _os
    _dbg = _os.environ.get("L0ADBG", "")
    lob = p.sbuf("l0a_lob", [128, 16], F32)
    if _dbg in ("nolob",):
        p.op("ve", lambda e: e.memset(lob[:, :], 0.0), writes=[lob])
    else:
        p.op("pe", lambda e: e.transpose(core.pmisc[0:1, 0:128], bis[0:128, 0:1],
             core.ident[:, :]), reads=[bis, core.ident], writes=[core.pmisc])
        p.op("ve", lambda e: e.tensor_copy(out=core.dump[0:1, 0:128],
             in_=core.pmisc[0:1, 0:128]), reads=[core.pmisc], writes=[core.wstage[0]])
        p.op("ve", lambda e: e.tensor_copy(
            out=core.dump.h[0:1, 256:272].rearrange("a (c b) -> a c b", b=1),
            in_=core.dump.h[0:1, 0:128].rearrange("a (c b) -> a c b", c=16)[:, :, 0:1]),
            reads=[core.wstage[0]], writes=[core.wstage[0]])
        p.op("pe", lambda e: e.matmul(core.pmisc[:, 16:32], core.onesf[0:1, 0:128],
             core.dump[0:1, 256:272], start=True, stop=True),
             reads=[core.onesf, core.wstage[0]], writes=[core.pmisc])
        p.op("ve", lambda e: e.tensor_scalar(out=lob[:, :], in0=core.pmisc[:, 16:32],
             scalar1=float(0.125 * 0.125), scalar2=None, op0=ALU.mult),
             reads=[core.pmisc], writes=[lob])

    emit_xt_rows(core, x_src, xt, 0, row0=0)
    for oc in range(FC):
        stream_wcols(core, W["WqT"], 1024, oc * 128, (oc + 1) * 128, oc % 2,
                     colsum=True, bias=W["bq"])
        wb = core.wbf[oc % 2]
        cr = core.colr[oc % 2]
        cb = core.colb[oc % 2]
        for s in range(NS):
            pp = core.bank()
            sl = slice(s * 512, (s + 1) * 512)
            mm_chain(p, pp[:, 0:512],
                     [(wb[:, k * 128:(k + 1) * 128],
                       xt[:, k * TOK + s * 512:k * TOK + (s + 1) * 512]) for k in range(FC)],
                     reads=[wb, cr, cb, core.rows, core.onesrow, xt], writes=[pp],
                     extras=[(cr[0:1, 0:128], core.rows[0:1, sl]),
                             (cb[0:1, 0:128], core.onesrow[0:1, 0:512])])
            p.op("ve", lambda e, oc=oc, s=s, pp=pp: e.tensor_scalar(
                out=qdm[:, oc * TOK + s * 512:oc * TOK + (s + 1) * 512],
                in0=pp[:, 0:512], scalar1=0.125, scalar2=None, op0=ALU.mult),
                reads=[pp], writes=[qdm])
            if _dbg == "nonorm":
                continue
            p.op("se", lambda e, oc=oc, s=s: e.activation(
                out=stg[:, s * 512:(s + 1) * 512],
                in_=qdm[:, oc * TOK + s * 512:oc * TOK + (s + 1) * 512],
                func=AF.Square), reads=[qdm], writes=[stg])
        if _dbg == "nonorm":
            continue
        for s4 in range(4):
            p.op("pe", lambda e, s4=s4: e.matmul(core.pmisc[0:2, 0:512],
                 core.e2f[:, 0:2], stg[:, s4 * 512:(s4 + 1) * 512],
                 start=True, stop=True), reads=[stg, core.e2f], writes=[core.pmisc])
            p.op("ve", lambda e, s4=s4: e.tensor_copy(
                out=msb[0:2, s4 * 512:(s4 + 1) * 512],
                in_=core.pmisc[0:2, 0:512]), reads=[core.pmisc], writes=[msb])
        for qi in range(NT):
            p.op("pe", lambda e, qi=qi: e.transpose(core.ptr[0:128, 0:2],
                 msb[0:2, qi * 128:(qi + 1) * 128], core.ident[0:2, 0:2]),
                 reads=[msb, core.ident], writes=[core.ptr])
            p.op("ve", lambda e, qi=qi, oc=oc: e.tensor_tensor(
                out=mcols.h[:, qi * 16 + 2 * oc:qi * 16 + 2 * oc + 2],
                in0=core.ptr[0:128, 0:2],
                in1=lob[:, 2 * oc:2 * oc + 2], op=ALU.is_gt),
                reads=[core.ptr, lob], writes=[mcols])

    p.barrier()
    out_fm = Buf(qdm.h, "l0a_out"); p.bufs.append(out_fm)
    emit_flash(core, qdm, kd, vd, acc, nml, out_fm, qoff,
               pieces=[(0, 0, 64)], stair=None, causal=False,
               out_mask=(mcols, 16),
               kslab_bufs=(KSlabSlice(xt, 0), KSlabSlice(xt, 4096)),
               pstage=pstage, ptst=ptst)

    for ob in range(2):
        which = ob % 2
        stream_wcols(core, W["WoT"], 1024, ob * 512, (ob + 1) * 512, which,
                     colsum=False, bias=W["bo"])
        wb = core.wbf[which]
        cb = core.colb[which]
        for i in range(NT):
            buf = core.xld[0]
            p.dma("sy", buf[:, :], x_src.ap()[i * 128:(i + 1) * 128, :],
                  reads=[x_src], writes=[buf])
            pp = core.bank()
            mm_chain(p, pp[:, 0:512],
                     [(out_fm.h[:, k * TOK + i * 128:k * TOK + (i + 1) * 128],
                       wb[:, k * 512:(k + 1) * 512]) for k in range(FC)],
                     reads=[out_fm, wb, cb, core.onesrow], writes=[pp],
                     extras=[(core.onesrow[0:1, 0:128], cb[0:1, 0:512])])
            p.op("ve", lambda e, pp=pp, buf=buf, ob=ob: e.tensor_tensor(
                out=core.dump[:, 0:512], in0=pp[:, 0:512],
                in1=buf[:, ob * 512:(ob + 1) * 512], op=ALU.add),
                reads=[pp, buf], writes=[core.wstage[0]])
            p.dma("gp", x_dst.ap()[i * 128:(i + 1) * 128, ob * 512:(ob + 1) * 512],
                  core.dump[:, 0:512], reads=[core.wstage[0]], writes=[x_dst],
                  is_output=True)


def e2f_slice(core):
    return core.e2f[:, 0:2]


L0ATTN_W = dict(WqT=[1024, 1024], WkT=[1024, 1024], WvT=[1024, 1024],
                WoT=[1024, 1024], bq=[1024], bk=[1024], bv=[1024], bo=[1024],
                G16=[128, 16], G16T=[16, 128])


def build_l0_attn_program(qoff=0):
    p = Prog("l0attn")
    xf = p.dram("xf", [SEQ, C], F32, "ExternalInput")
    x_in = p.dram("x_in", [TOK, C], F32, "ExternalInput")
    x_out = p.dram("x_out", [TOK, C], F32, "ExternalOutput")
    cident = p.dram("cident", [128, 128], F32, "ExternalInput")
    W = {k: p.dram(k, v, F32, "ExternalInput") for k, v in L0ATTN_W.items()}
    W["_kd"] = p.dram("kd", [1024, SEQ], BF16)
    W["_vd"] = p.dram("vd", [SEQ, 1024], BF16)
    W["_nd"] = p.dram("nd", [16, SEQ], F32)
    W["_md"] = p.dram("md", [16, SEQ], F32)
    core = Core(p)
    core.e2f = core.p.sbuf("c_e2f", [128, 2], F32)
    p.op("ve", lambda e: e.memset(core.e2f[:, 0:2], 0.0), writes=[core.e2f])
    p.op("ve", lambda e: e.memset(core.e2f[0:64, 0:1], 1.0), writes=[core.e2f])
    p.op("ve", lambda e: e.memset(core.e2f[64:128, 1:2], 1.0), writes=[core.e2f])
    emit_load_ident(core, cident)
    emit_l0_attn(core, W, x_in, x_out, xf, 0)
    return p.build()


def host_g16():
    G = np.zeros((128, 16), np.float32)
    for pp in range(128):
        G[pp, pp // 8] = 1.0
    return G


def host_l0_attn_weights(inp):
    T = lambda a: np.ascontiguousarray(np.asarray(a).T.astype(np.float32))
    A = lambda k: np.asarray(inp[k], np.float32)
    return dict(WqT=T(inp["L0_attn_Wq"]), WkT=T(inp["L0_attn_Wk"]),
                WvT=T(inp["L0_attn_Wv"]), WoT=T(inp["L0_attn_Wo"]),
                bq=A("L0_attn_bq"), bk=A("L0_attn_bk"), bv=A("L0_attn_bv"),
                bo=A("L0_attn_bo"), G16=host_g16(),
                G16T=np.ascontiguousarray(host_g16().T))



NCH = 16
DIN = 2048
NH = 32
XBC_S = 2052


def emit_l1_attn(core, W, x_src, x_dst, xprev3, mode, cdall=None, me=0):
    p = core.p
    xt = p.sbuf("m_xt", [128, FC * TOK], BF16)
    dtf = p.sbuf("m_dtf", [128, 4 * TOK], F32)
    dtT = p.sbuf("m_dtT", [128, 512], F32)
    ncT = p.sbuf("m_ncT", [128, 512], F32)
    dtdsT = p.sbuf("m_dtds", [128, 512], F32)
    sdoT = p.sbuf("m_sdo", [128, 512], F32)
    clb = p.sbuf("m_clb", [128, 512], F32)
    dab = p.sbuf("m_dab", [128, 512], F32)
    wcv = p.sbuf("m_wcv", [128, 96], F32)
    dcol = p.sbuf("m_dcol", [128, 32], F32)
    uht = p.sbuf("m_uht", [128, 24], BF16)
    hst = p.sbuf("m_hst", [3, 8], F32)
    hrow3 = p.sbuf("m_hrow3", [3, 1024], F32)
    hrows = p.sbuf("m_hrows", [1, 4], BF16)
    state = p.sbuf("m_state", [64, 2048], F32)
    cstg = p.sbuf("m_cstg", [64, 2048], F32)
    statb = p.sbuf("m_statb", [64, 2048], BF16)
    cfm = p.sbuf("m_cfm", [64, 128], BF16)
    gsb = p.sbuf("m_gsb", [128, 128], BF16)
    bcsb = p.sbuf("m_bc", [128, 128], BF16)
    btm = p.sbuf("m_btm", [128, 64], BF16)
    xfm = p.sbuf("m_xfm", [64, 128], BF16)
    xtm = p.sbuf("m_xtm", [128, 64], BF16)
    xdt = p.sbuf("m_xdt", [128, 64], BF16)
    xds = p.sbuf("m_xds", [128, 64], BF16)
    est = p.sbuf("m_est", [128, 128], BF16)
    mneg = p.sbuf("m_mneg", [128, 128], F32)
    ytile = p.sbuf("m_yt", [128, 2048], F32)
    ysc = p.sbuf("m_ysc", [128, 64], F32)
    stg = p.sbuf("m_stg", [128, 512], BF16)
    xbc_d = W["_xbc"]
    z_d = W["_z"]

    p.dma("sy", mneg[:, :], W["mneg"].ap()[:, :], reads=[W["mneg"]], writes=[mneg])
    p.dma("gp", wcv.h[:, 0:68].rearrange("p (k j) -> p k j", k=17),
          W["convw"].ap()[:, :].rearrange("(k p) j -> p k j", p=128),
          reads=[W["convw"]], writes=[wcv], allow_slow_non_contiguous=True)
    load_fmcol(core, wcv, W["convb"], 2176, 68)
    p.dma("gp", wcv[0:32, 85:86], W["dt_bias"].ap()[:].rearrange("(p a) -> p a", a=1),
          reads=[W["dt_bias"]], writes=[wcv])
    p.dma("gp", wcv[0:32, 86:87], W["A_log"].ap()[:].rearrange("(p a) -> p a", a=1),
          reads=[W["A_log"]], writes=[wcv])
    p.dma("gp", wcv[0:32, 87:88], W["Dv"].ap()[:].rearrange("(p a) -> p a", a=1),
          reads=[W["Dv"]], writes=[wcv])
    p.op("se", lambda e: e.activation(out=wcv[0:32, 88:89], in_=wcv[0:32, 86:87],
         func=AF.Exp), reads=[wcv], writes=[wcv])
    p.op("ve", lambda e: e.tensor_scalar(out=wcv[0:32, 88:89], in0=wcv[0:32, 88:89],
         scalar1=-1.0, scalar2=None, op0=ALU.mult), reads=[wcv], writes=[wcv])
    p.op("ve", lambda e: e.tensor_copy(out=core.dump[0:32, 0:1], in_=wcv[0:32, 87:88]),
         reads=[wcv], writes=[core.wstage[0]])
    p.op("pe", lambda e: e.transpose(core.pmisc[0:1, 0:32], core.dump[0:32, 0:1],
         core.ident[0:32, 0:32]), reads=[core.wstage[0], core.ident], writes=[core.pmisc])
    p.op("ve", lambda e: e.tensor_copy(out=core.dump[0:1, 0:32], in_=core.pmisc[0:1, 0:32]),
         reads=[core.pmisc], writes=[core.wstage[0]])
    p.op("pe", lambda e: e.matmul(core.pmisc[:, 32:64], core.onesf[0:1, 0:128],
         core.dump[0:1, 0:32], start=True, stop=True),
         reads=[core.onesf, core.wstage[0]], writes=[core.pmisc])
    p.op("ve", lambda e: e.tensor_copy(out=dcol[:, :], in_=core.pmisc[:, 32:64]),
         reads=[core.pmisc], writes=[dcol])

    p.dma("sy", hrow3[:, :], xprev3.ap()[:, :], reads=[xprev3], writes=[hrow3])
    p.op("ve", lambda e: e.memset(hst[:, :], 0.0), writes=[hst])
    p.op("se", lambda e: e.activation(out=core.dump[0:3, 0:1024], in_=hrow3[:, :],
         func=AF.Copy, accum_out=hst[:, 0:1]), reads=[hrow3], writes=[core.wstage[0], hst])
    p.op("se", lambda e: e.activation(out=core.dump[0:3, 0:1024], in_=hrow3[:, :],
         func=AF.Square, accum_out=hst[:, 1:2]), reads=[hrow3], writes=[core.wstage[0], hst])
    p.op("ve", lambda e: e.tensor_scalar(out=hst[:, 2:4], in0=hst[:, 0:2],
         scalar1=1.0 / C, scalar2=None, op0=ALU.mult), reads=[hst], writes=[hst])
    p.op("ve", lambda e: e.tensor_tensor(out=hst[:, 4:5], in0=hst[:, 2:3],
         in1=hst[:, 2:3], op=ALU.mult), reads=[hst], writes=[hst])
    p.op("ve", lambda e: e.tensor_tensor(out=hst[:, 4:5], in0=hst[:, 3:4],
         in1=hst[:, 4:5], op=ALU.subtract), reads=[hst], writes=[hst])
    p.op("se", lambda e: e.activation(out=hst[:, 5:6], in_=hst[:, 4:5],
         func=AF.Ln, bias=core.epscol[0:3, 0:1]), reads=[hst, core.epscol], writes=[hst])
    p.op("se", lambda e: e.activation(out=hst[:, 6:7], in_=hst[:, 5:6],
         func=AF.Exp, scale=-0.5), reads=[hst], writes=[hst])
    p.op("ve", lambda e: e.tensor_scalar(out=core.dump[0:3, 0:1024], in0=hrow3[:, :],
         scalar1=hst[:, 2:3], scalar2=hst[:, 6:7], op0=ALU.subtract, op1=ALU.mult),
         reads=[hrow3, hst], writes=[core.wstage[0]])
    p.op("ve", lambda e: e.tensor_copy(out=hrow3[:, :], in_=core.dump[0:3, 0:1024]),
         reads=[core.wstage[0]], writes=[hrow3])
    for fc in range(FC):
        p.op("pe", lambda e, fc=fc: e.transpose(core.ptr[0:128, 0:3],
             hrow3[0:3, fc * 128:(fc + 1) * 128], core.ident[0:3, 0:3]),
             reads=[hrow3, core.ident], writes=[core.ptr])
        p.op("ve", lambda e, fc=fc: e.tensor_copy(out=uht[:, fc * 3:(fc + 1) * 3],
             in_=core.ptr[0:128, 0:3]), reads=[core.ptr], writes=[uht])
    p.op("ve", lambda e: e.scalar_tensor_tensor(out=hst[:, 7:8], in0=hst[:, 2:3],
         scalar=-1.0, in1=hst[:, 6:7], op0=ALU.mult, op1=ALU.mult),
         reads=[hst], writes=[hst])
    p.op("pe", lambda e: e.transpose(core.pmisc[0:1, 0:3], hst[0:3, 7:8],
         core.ident[0:3, 0:3]), reads=[hst, core.ident], writes=[core.pmisc])
    p.op("ve", lambda e: e.tensor_copy(out=hrows[0:1, 0:3], in_=core.pmisc[0:1, 0:3]),
         reads=[core.pmisc], writes=[hrows])

    emit_stats(core, x_src)
    emit_xt(core, x_src, xt)

    for oc in range(17):
        which = oc % 2
        stream_wcols(core, W["W_inT"], 1024, 2048 + oc * 128, 2048 + (oc + 1) * 128,
                     which, colsum=True)
        wb = core.wbf[which]
        cr = core.colr[which]
        pph = core.pmisc
        mm_chain(p, pph[:, 0:3],
                 [(wb[:, k * 128:(k + 1) * 128], uht[:, k * 3:(k + 1) * 3]) for k in range(FC)],
                 reads=[wb, cr, uht, hrows], writes=[pph],
                 extras=[(cr[0:1, 0:128], hrows[0:1, 0:3])])
        p.op("ve", lambda e, oc=oc: e.tensor_copy(out=stg[:, 0:3], in_=pph[:, 0:3]),
             reads=[pph], writes=[stg])
        p.dma("gp", xbc_d.ap()[oc * 128:(oc + 1) * 128, 1:4], stg[:, 0:3],
              reads=[stg], writes=[xbc_d])
        for s in range(NS):
            pp = core.bank()
            sl = slice(s * 512, (s + 1) * 512)
            mm_chain(p, pp[:, 0:512],
                     [(wb[:, k * 128:(k + 1) * 128],
                       xt[:, k * TOK + s * 512:k * TOK + (s + 1) * 512]) for k in range(FC)],
                     reads=[wb, cr, core.rows, xt], writes=[pp],
                     extras=[(cr[0:1, 0:128], core.rows[0:1, sl])])
            p.op("ve", lambda e, pp=pp: e.tensor_copy(out=stg[:, :], in_=pp[:, 0:512]),
                 reads=[pp], writes=[stg])
            p.dma("gp", xbc_d.ap()[oc * 128:(oc + 1) * 128, 4 + s * 512:4 + (s + 1) * 512],
                  stg[:, :], reads=[stg], writes=[xbc_d])
    cacc = Buf(core.g32[0].h, "m_cacc") if False else None
    cin = p.sbuf("m_cin", [128, XBC_S], BF16)
    cout = p.sbuf("m_cout", [128, 2048], F32)
    for oc in range(17):
        p.dma("sy", cin[:, :], xbc_d.ap()[oc * 128:(oc + 1) * 128, :],
              reads=[xbc_d], writes=[cin])
        p.op("ve", lambda e, oc=oc: e.tensor_scalar(out=cout[:, :],
             in0=cin[:, 1:2049], scalar1=wcv[:, oc * 4:oc * 4 + 1], scalar2=None,
             op0=ALU.mult), reads=[cin, wcv], writes=[cout])
        for j in range(1, 4):
            p.op("ve", lambda e, oc=oc, j=j: e.scalar_tensor_tensor(out=cout[:, :],
                 in0=cin[:, 1 + j:2049 + j], scalar=wcv[:, oc * 4 + j:oc * 4 + j + 1],
                 in1=cout[:, :], op0=ALU.mult, op1=ALU.add),
                 reads=[cin, wcv, cout], writes=[cout])
        p.op("se", lambda e, oc=oc: e.activation(out=cin[:, 4:2052], in_=cout[:, :],
             func=AF.Silu, bias=wcv[:, 68 + oc:69 + oc]), reads=[cout, wcv], writes=[cin])
        p.dma("gp", xbc_d.ap()[oc * 128:(oc + 1) * 128, 4:2052], cin[:, 4:2052],
              reads=[cin], writes=[xbc_d])

    wdt = 4224
    stream_wcols(core, W["W_inT"], 1024, wdt, wdt + 32, 0, colsum=True)
    wb = core.wbf[0]
    cr = core.colr[0]
    for s in range(NS):
        pp = core.bank()
        sl = slice(s * 512, (s + 1) * 512)
        mm_chain(p, pp[0:32, 0:512],
                 [(wb[:, k * 32:(k + 1) * 32],
                   xt[:, k * TOK + s * 512:k * TOK + (s + 1) * 512]) for k in range(FC)],
                 reads=[wb, cr, core.rows, xt], writes=[pp],
                 extras=[(cr[0:1, 0:32], core.rows[0:1, sl])])
        p.op("ve", lambda e, s=s, pp=pp: e.tensor_copy(
            out=dtf[0:32, s * 512:(s + 1) * 512], in_=pp[0:32, 0:512]),
            reads=[pp], writes=[dtf])
    T2 = TOK
    p.op("ve", lambda e: e.tensor_scalar(out=dtf[0:32, 0:T2], in0=dtf[0:32, 0:T2],
         scalar1=wcv[0:32, 85:86], scalar2=None, op0=ALU.add),
         reads=[dtf, wcv], writes=[dtf])
    p.op("se", lambda e: e.activation(out=dtf[0:32, T2:2 * T2], in_=dtf[0:32, 0:T2],
         func=AF.Abs), reads=[dtf], writes=[dtf])
    p.op("se", lambda e: e.activation(out=dtf[0:32, T2:2 * T2], in_=dtf[0:32, T2:2 * T2],
         func=AF.Exp, scale=-1.0), reads=[dtf], writes=[dtf])
    p.op("se", lambda e: e.activation(out=dtf[0:32, T2:2 * T2], in_=dtf[0:32, T2:2 * T2],
         func=AF.Ln, bias=core.onescolF[0:32, 0:1]), reads=[dtf, core.onescolF],
         writes=[dtf])
    p.op("ve", lambda e: e.tensor_scalar(out=dtf[0:32, 0:T2], in0=dtf[0:32, 0:T2],
         scalar1=0.0, scalar2=None, op0=ALU.max), reads=[dtf], writes=[dtf])
    p.op("ve", lambda e: e.tensor_tensor(out=dtf[0:32, 0:T2], in0=dtf[0:32, 0:T2],
         in1=dtf[0:32, T2:2 * T2], op=ALU.add), reads=[dtf], writes=[dtf])
    p.op("ve", lambda e: e.tensor_scalar(out=dtf[0:32, T2:2 * T2], in0=dtf[0:32, 0:T2],
         scalar1=wcv[0:32, 88:89], scalar2=None, op0=ALU.mult),
         reads=[dtf, wcv], writes=[dtf])
    src_off, dst_off = T2, 2 * T2
    for k in (1, 2, 4, 8, 16, 32, 64):
        p.op("ve", lambda e, k=k, so=src_off, do=dst_off: e.tensor_copy(
            out=dtf.h[0:32, do:do + T2], in_=dtf.h[0:32, so:so + T2]),
            reads=[dtf], writes=[dtf])
        p.op("ve", lambda e, k=k, so=src_off, do=dst_off: e.tensor_tensor(
            out=dtf.h[0:32, do:do + T2].rearrange("p (c t) -> p c t", c=NCH)[:, :, k:128],
            in0=dtf.h[0:32, so:so + T2].rearrange("p (c t) -> p c t", c=NCH)[:, :, 0:128 - k],
            in1=dtf.h[0:32, do:do + T2].rearrange("p (c t) -> p c t", c=NCH)[:, :, k:128],
            op=ALU.add), reads=[dtf], writes=[dtf])
        src_off, dst_off = dst_off, src_off
    cum_off = src_off
    for c in range(NCH):
        p.op("pe", lambda e, c=c: e.transpose(core.ptr[0:128, 0:32],
             dtf[0:32, c * 128:(c + 1) * 128], core.ident[0:32, 0:32]),
             reads=[dtf, core.ident], writes=[core.ptr])
        p.op("ve", lambda e, c=c: e.tensor_copy(out=dtT[:, c * 32:(c + 1) * 32],
             in_=core.ptr[0:128, 0:32]), reads=[core.ptr], writes=[dtT])
        p.op("pe", lambda e, c=c, co=cum_off: e.transpose(core.ptr[0:128, 0:32],
             dtf[0:32, co + c * 128:co + (c + 1) * 128], core.ident[0:32, 0:32]),
             reads=[dtf, core.ident], writes=[core.ptr])
        p.op("ve", lambda e, c=c: e.tensor_scalar(out=ncT[:, c * 32:(c + 1) * 32],
             in0=core.ptr[0:128, 0:32], scalar1=-1.0, scalar2=None, op0=ALU.mult),
             reads=[core.ptr], writes=[ncT])
    p.op("ve", lambda e, co=cum_off: e.tensor_copy(
        out=core.dump.h[0:32, 0:16].rearrange("p (c a) -> p c a", a=1),
        in_=dtf.h[0:32, co:co + T2].rearrange("p (c t) -> p c t", c=NCH)[:, :, 127:128]),
        reads=[dtf], writes=[core.wstage[0]])
    p.op("pe", lambda e: e.transpose(core.pmisc[0:16, 0:32], core.dump[0:32, 0:16],
         core.ident[0:32, 0:32]), reads=[core.wstage[0], core.ident], writes=[core.pmisc])
    p.op("ve", lambda e: e.tensor_copy(out=core.tmp32[0:16, 0:32],
         in_=core.pmisc[0:16, 0:32]), reads=[core.pmisc], writes=[core.tmp32])
    p.dma("gp", core.rb_dram.ap()[0:16, 0:32], core.tmp32.h[0:16, 0:32],
          reads=[core.tmp32], writes=[core.rb_dram])
    p.dma("gp", core.rowf.h[0:1, 0:512].rearrange("a (c h) -> a c h", c=16),
          core.rb_dram.ap()[0:16, 0:32].rearrange("(a c) h -> a c h", a=1),
          reads=[core.rb_dram], writes=[core.rowf])
    p.op("pe", lambda e: e.matmul(core.pmisc[:, 0:512], core.onesf[0:1, 0:128],
         core.rowf[0:1, 0:512], start=True, stop=True),
         reads=[core.onesf, core.rowf], writes=[core.pmisc])
    p.op("ve", lambda e: e.tensor_copy(out=clb[:, :], in_=core.pmisc[:, 0:512]),
         reads=[core.pmisc], writes=[clb])
    p.op("ve", lambda e: e.tensor_tensor(out=dtdsT[:, :], in0=clb[:, :], in1=ncT[:, :],
         op=ALU.add), reads=[clb, ncT], writes=[dtdsT])
    p.op("se", lambda e: e.activation(out=dtdsT[:, :], in_=dtdsT[:, :], func=AF.Exp),
         reads=[dtdsT], writes=[dtdsT])
    p.op("se", lambda e: e.activation(out=sdoT[:, :], in_=ncT[:, :], func=AF.Exp,
         scale=-1.0), reads=[ncT], writes=[sdoT])
    p.op("se", lambda e: e.activation(out=dab[:, :], in_=clb[:, :], func=AF.Exp),
         reads=[clb], writes=[dab])
    p.op("ve", lambda e: e.tensor_tensor(out=dtdsT[:, :], in0=dtdsT[:, :], in1=dtT[:, :],
         op=ALU.mult), reads=[dtT, dtdsT], writes=[dtdsT])

    p.op("ve", lambda e: e.memset(state[:, :], 0.0), writes=[state])
    if mode == "full":
        mw = p.sbuf("m_mw", [1, 4], F32)
        mcolb = p.sbuf("m_mcolb", [64, 4], F32)
        p.dma("sy", mw[0:1, :], W["mewt"].ap()[0:1, :], reads=[W["mewt"]], writes=[mw])
        p.op("pe", lambda e: e.matmul(core.pmisc[0:64, 60:64], core.onesf[0:1, 0:64],
             mw[0:1, :], start=True, stop=True), reads=[core.onesf, mw],
             writes=[core.pmisc])
        p.op("ve", lambda e: e.tensor_copy(out=mcolb[:, :], in_=core.pmisc[0:64, 60:64]),
             reads=[core.pmisc], writes=[mcolb])
        for j in range(4):
            p.dma("sy", core.rowf[0:1, 0:32],
                  cdall.ap()[j * 2080 + 2048:j * 2080 + 2080, 0:1].rearrange("(a h) b -> a (h b)", a=1),
                  reads=[cdall], writes=[core.rowf], allow_slow_non_contiguous=True)
            p.op("pe", lambda e: e.matmul(core.pmisc[0:64, 0:32], core.onesf[0:1, 0:64],
                 core.rowf[0:1, 0:32], start=True, stop=True),
                 reads=[core.onesf, core.rowf], writes=[core.pmisc])
            p.op("ve", lambda e, j=j: e.tensor_scalar(out=core.dump[0:64, 0:32],
                 in0=core.pmisc[0:64, 0:32], scalar1=mcolb[:, j:j + 1], scalar2=None,
                 op0=ALU.mult), reads=[core.pmisc, mcolb], writes=[core.wstage[0]])
            p.op("ve", lambda e, j=j: e.tensor_scalar(out=core.dump[0:64, 32:33],
                 in0=mcolb[:, j:j + 1],
                 scalar1=-1.0, scalar2=1.0, op0=ALU.mult, op1=ALU.add),
                 reads=[mcolb], writes=[core.wstage[0]])
            p.op("ve", lambda e: e.tensor_scalar(out=core.dump[0:64, 0:32],
                 in0=core.dump[0:64, 0:32], scalar1=core.dump[0:64, 32:33],
                 scalar2=None, op0=ALU.add), reads=[core.wstage[0]],
                 writes=[core.wstage[0]])
            for h in range(NH):
                p.op("ve", lambda e, h=h: e.tensor_scalar(
                    out=state[:, h * 64:(h + 1) * 64], in0=state[:, h * 64:(h + 1) * 64],
                    scalar1=core.dump[0:64, h:h + 1], scalar2=None, op0=ALU.mult),
                    reads=[state, core.wstage[0]], writes=[state])
            p.dma("sy", cstg.h[:, :].rearrange("n (h p) -> n h p", h=NH),
                  cdall.ap()[j * 2080:j * 2080 + 2048, :].rearrange("(h n) p -> n h p", n=64),
                  reads=[cdall], writes=[cstg])
            p.op("ve", lambda e, j=j: e.tensor_scalar(out=cstg[:, :], in0=cstg[:, :],
                 scalar1=mcolb[:, j:j + 1], scalar2=None, op0=ALU.mult),
                 reads=[cstg, mcolb], writes=[cstg])
            p.op("ve", lambda e: e.tensor_tensor(out=state[:, :], in0=state[:, :],
                 in1=cstg[:, :], op=ALU.add), reads=[cstg, state], writes=[state])

    if mode == "full":
        for oc in range(16):
            which = oc % 2
            stream_wcols(core, W["W_inT"], 1024, oc * 128, (oc + 1) * 128, which,
                         colsum=True)
            wb = core.wbf[which]
            cr = core.colr[which]
            for i in range(NT):
                pp = core.bank()
                mm_chain(p, pp[:, 0:128],
                         [(xt[:, k * TOK + i * 128:k * TOK + (i + 1) * 128],
                           wb[:, k * 128:(k + 1) * 128]) for k in range(FC)],
                         reads=[wb, cr, core.rows, xt], writes=[pp],
                         extras=[(core.rows[0:1, i * 128:(i + 1) * 128], cr[0:1, 0:128])])
                p.op("se", lambda e, pp=pp: e.activation(out=stg[:, 0:128],
                     in_=pp[:, 0:128], func=AF.Silu), reads=[pp], writes=[stg])
                p.dma("gp", z_d.ap()[i * 128:(i + 1) * 128, oc * 128:(oc + 1) * 128],
                      stg[:, 0:128], reads=[stg], writes=[z_d])

    for c in range(NCH):
        if mode == "full":
            p.op("ve", lambda e: e.tensor_copy(out=statb[:, :], in_=state[:, :]),
                 reads=[state], writes=[statb])
        p.dma("sy", bcsb[:, :], xbc_d.ap()[2048:2176, 4 + c * 128:4 + (c + 1) * 128],
              reads=[xbc_d], writes=[bcsb])
        p.op("ve", lambda e: e.tensor_copy(out=cfm[:, :], in_=bcsb.h[64:128, :]),
             reads=[bcsb], writes=[cfm])
        ppg = core.bank()
        p.op("pe", lambda e, ppg=ppg: e.matmul(ppg[:, 0:128], bcsb.h[0:64, :],
             cfm[:, :], start=True, stop=True), reads=[bcsb, cfm], writes=[ppg])
        p.op("ve", lambda e, ppg=ppg: e.tensor_copy(out=gsb[:, :], in_=ppg[:, 0:128]),
             reads=[ppg], writes=[gsb])
        p.op("pe", lambda e: e.transpose(core.ptr.h[:, :].bitcast(BF16)[:, 0:64],
             bcsb.h[0:64, :], core.identb[0:64, 0:64]),
             reads=[bcsb, core.identb], writes=[core.ptr])
        p.op("ve", lambda e: e.tensor_copy(out=btm[:, :],
             in_=core.ptr.h[:, :].bitcast(BF16)[:, 0:64]), reads=[core.ptr], writes=[btm])
        for h in range(NH):
            col = c * 32 + h
            p.dma("sy", xfm[:, :], xbc_d.ap()[h * 64:(h + 1) * 64, 4 + c * 128:4 + (c + 1) * 128],
                  reads=[xbc_d], writes=[xfm])
            p.op("pe", lambda e: e.transpose(core.ptr.h[:, :].bitcast(BF16)[:, 0:64],
                 xfm[:, :], core.identb[0:64, 0:64]), reads=[xfm, core.identb],
                 writes=[core.ptr])
            p.op("ve", lambda e: e.tensor_copy(out=xtm[:, :],
                 in_=core.ptr.h[:, :].bitcast(BF16)[:, 0:64]), reads=[core.ptr],
                 writes=[xtm])
            p.op("ve", lambda e, col=col: e.tensor_scalar(out=xdt[:, :], in0=xtm[:, :],
                 scalar1=dtT[:, col:col + 1], scalar2=None, op0=ALU.mult),
                 reads=[xtm, dtT], writes=[xdt])
            p.op("ve", lambda e, col=col: e.tensor_scalar(out=xds[:, :], in0=xtm[:, :],
                 scalar1=dtdsT[:, col:col + 1], scalar2=None, op0=ALU.mult),
                 reads=[xtm, dtdsT], writes=[xds])
            if mode == "full":
                p.op("ve", lambda e, col=col: e.tensor_scalar(out=core.dump[:, 0:128],
                     in0=mneg[:, :], scalar1=ncT[:, col:col + 1], scalar2=None,
                     op0=ALU.subtract), reads=[mneg, ncT], writes=[core.wstage[0]])
                ppe = core.bank()
                p.op("pe", lambda e, ppe=ppe: e.transpose(ppe[:, 0:128],
                     core.dump[:, 0:128], core.ident[:, :]),
                     reads=[core.wstage[0], core.ident], writes=[ppe])
                p.op("se", lambda e, ppe=ppe, col=col: e.activation(out=est[:, :],
                     in_=ppe[:, 0:128], func=AF.Exp, bias=ncT[:, col:col + 1]),
                     reads=[ppe, ncT], writes=[est])
                p.op("ve", lambda e: e.tensor_tensor(out=est[:, :], in0=est[:, :],
                     in1=gsb[:, :], op=ALU.mult), reads=[gsb, est], writes=[est])
                ppy = core.bank()
                p.op("pe", lambda e, ppy=ppy: e.matmul(ppy[:, 0:64], est[:, :],
                     xdt[:, :], start=True, stop=True), reads=[est, xdt], writes=[ppy])
                ppo = core.bank()
                p.op("pe", lambda e, ppo=ppo, h=h: e.matmul(ppo[:, 0:64],
                     cfm[:, :], statb.h[:, h * 64:(h + 1) * 64],
                     start=True, stop=True), reads=[cfm, statb], writes=[ppo])
                p.op("ve", lambda e, ppo=ppo, col=col: e.tensor_scalar(out=ysc[:, :],
                     in0=ppo[:, 0:64], scalar1=sdoT[:, col:col + 1], scalar2=None,
                     op0=ALU.mult), reads=[ppo, sdoT], writes=[ysc])
                p.op("ve", lambda e, h=h: e.scalar_tensor_tensor(out=ysc[:, :],
                     in0=xtm[:, :], scalar=dcol[:, h:h + 1], in1=ysc[:, :],
                     op0=ALU.mult, op1=ALU.add), reads=[xtm, dcol, ysc], writes=[ysc])
                p.op("ve", lambda e, ppy=ppy, h=h: e.scalar_tensor_tensor(
                    out=ytile[:, h * 64:(h + 1) * 64], in0=ppy[:, 0:64], scalar=1.0,
                    in1=ysc[:, :], op0=ALU.mult, op1=ALU.add),
                    reads=[ppy, ysc], writes=[ytile])
            pps = core.bank()
            p.op("pe", lambda e, pps=pps: e.matmul(pps[0:64, 0:64], btm[:, :],
                 xds[:, :], start=True, stop=True), reads=[btm, xds], writes=[pps])
            p.op("ve", lambda e, pps=pps, h=h, col=col: e.scalar_tensor_tensor(
                out=state[:, h * 64:(h + 1) * 64], in0=state[:, h * 64:(h + 1) * 64],
                scalar=dab[0:64, col:col + 1], in1=pps[0:64, 0:64],
                op0=ALU.mult, op1=ALU.add), reads=[pps, dab, state], writes=[state])
        if mode == "full":
            zt = cin
            p.dma("sy", zt[:, 0:2048], z_d.ap()[c * 128:(c + 1) * 128, :],
                  reads=[z_d], writes=[zt])
            p.op("ve", lambda e: e.tensor_tensor(out=ytile[:, :], in0=ytile[:, :],
                 in1=zt[:, 0:2048], op=ALU.mult), reads=[zt, ytile], writes=[ytile])
            p.op("se", lambda e: e.activation(out=cout[:, :], in_=ytile[:, :],
                 func=AF.Square, accum_out=core.diag[:, 12:13]),
                 reads=[ytile], writes=[cout, core.diag])
            p.op("ve", lambda e: e.tensor_scalar(out=core.diag[:, 12:13],
                 in0=core.diag[:, 12:13], scalar1=1.0 / DIN, scalar2=None,
                 op0=ALU.mult), reads=[core.diag], writes=[core.diag])
            p.op("se", lambda e: e.activation(out=core.diag[:, 12:13],
                 in_=core.diag[:, 12:13], func=AF.Ln, bias=core.epscol[:, 0:1]),
                 reads=[core.diag, core.epscol], writes=[core.diag])
            p.op("se", lambda e: e.activation(out=core.diag[:, 12:13],
                 in_=core.diag[:, 12:13], func=AF.Exp, scale=-0.5),
                 reads=[core.diag], writes=[core.diag])
            p.op("ve", lambda e, c=c: e.tensor_scalar(out=ytile[:, :], in0=ytile[:, :],
                 scalar1=core.diag[:, 12:13], scalar2=None, op0=ALU.mult),
                 reads=[core.diag, ytile], writes=[ytile])
            for b in range(16):
                p.op("pe", lambda e, b=b: e.transpose(
                    core.ptr[:, 0:128], ytile[:, b * 128:(b + 1) * 128],
                    core.ident[:, :]), reads=[ytile, core.ident], writes=[core.ptr])
                p.op("ve", lambda e, b=b: e.tensor_copy(out=stg[:, 0:128],
                     in_=core.ptr[:, 0:128]), reads=[core.ptr], writes=[stg])
                p.dma("gp", W["_yg"].ap()[b * 128:(b + 1) * 128, c * 128:(c + 1) * 128],
                      stg[:, 0:128], reads=[stg], writes=[W["_yg"]])

    if mode == "pre":
        cd = W["_cd"]
        for h in range(NH):
            p.dma("gp", cd.ap()[h * 64:(h + 1) * 64, :].rearrange("n p -> p n") if False else cd.ap()[h * 64:(h + 1) * 64, :],
                  state.h[:, h * 64:(h + 1) * 64].rearrange("n p -> n p") if False else state.h[:, h * 64:(h + 1) * 64],
                  reads=[state], writes=[cd])
        p.op("ve", lambda e: e.memset(core.diag[:, 14:15], 0.0), writes=[core.diag])
        for c in range(NCH):
            p.op("ve", lambda e, c=c: e.tensor_tensor(out=core.diag[0:32, 14:15],
                 in0=clb.h[0:32, :].rearrange("p (c h) -> p c h", c=NCH)[:, c, :].rearrange("p h -> p h")[:, 0:1] if False else core.diag[0:32, 14:15],
                 in1=core.diag[0:32, 14:15], op=ALU.add), reads=[clb, core.diag],
                 writes=[core.diag]) if False else None
        p.op("se", lambda e, co=cum_off: e.activation(out=core.diag[0:32, 14:15],
             in_=dtf.h[0:32, co + T2 - 1:co + T2], func=AF.Exp),
             reads=[dtf], writes=[core.diag])
        p.dma("gp", cd.ap()[2048:2080, 0:1], core.diag.h[0:32, 14:15],
              reads=[core.diag], writes=[cd], allow_slow_non_contiguous=True)
        return

    p.barrier()
    dtb16 = Buf(dtf.h, "m_dtb16"); p.bufs.append(dtb16)
    for k in range(16):
        if k < 8:
            p.dma("sy", xt[:, k * TOK:(k + 1) * TOK],
                  W["_yg"].ap()[k * 128:(k + 1) * 128, :], reads=[W["_yg"]], writes=[xt])
        else:
            p.dma("sy", dtf.h[:, :].bitcast(BF16)[:, (k - 8) * TOK:(k - 7) * TOK],
                  W["_yg"].ap()[k * 128:(k + 1) * 128, :], reads=[W["_yg"]],
                  writes=[dtb16])

    def yg_block(k, i):
        if k < 8:
            return xt.h[:, k * TOK + i * 128:k * TOK + (i + 1) * 128]
        return dtf.h[:, :].bitcast(BF16)[:, (k - 8) * TOK + i * 128:(k - 8) * TOK + (i + 1) * 128]

    normw = p.sbuf("m_nw", [128, 16], F32)
    load_fmcol(core, normw, W["norm_w"], 2048, 0)
    for ob in range(4):
        which = ob % 2
        stream_wcols(core, W["W_outT"], 2048, ob * 256, (ob + 1) * 256, which,
                     colsum=False, gamma_col=(normw, 0))
        wb = core.wbf[which]
        for i in range(NT):
            buf = core.xld[0]
            p.dma("sy", buf[:, :], x_src.ap()[i * 128:(i + 1) * 128, :],
                  reads=[x_src], writes=[buf])
            pp = core.bank()
            mm_chain(p, pp[:, 0:256],
                     [(yg_block(k, i), wb[:, k * 256:(k + 1) * 256]) for k in range(16)],
                     reads=[xt, dtb16, wb], writes=[pp])
            p.op("ve", lambda e, pp=pp, buf=buf, ob=ob: e.tensor_tensor(
                out=core.dump[:, 0:256], in0=pp[:, 0:256],
                in1=buf[:, ob * 256:(ob + 1) * 256], op=ALU.add),
                reads=[pp, buf], writes=[core.wstage[0]])
            p.dma("gp", x_dst.ap()[i * 128:(i + 1) * 128, ob * 256:(ob + 1) * 256],
                  core.dump[:, 0:256], reads=[core.wstage[0]], writes=[x_dst],
                  is_output=True)


L1ATTN_W = dict(W_inT=[1024, 4256], convw=[2176, 4], convb=[2176], dt_bias=[32],
                A_log=[32], Dv=[32], norm_w=[2048], W_outT=[2048, 1024],
                mneg=[128, 128])


def build_l1_attn_program(mode, me=0):
    p = Prog("l1attn" + mode)
    x_in = p.dram("x_in", [TOK, C], F32, "ExternalInput")
    xprev3 = p.dram("xprev3", [3, C], F32, "ExternalInput")
    cident = p.dram("cident", [128, 128], F32, "ExternalInput")
    W = {k: p.dram(k, v, F32, "ExternalInput") for k, v in L1ATTN_W.items()}
    W["_xbc"] = p.dram("xbcd", [2176, XBC_S], BF16)
    W["_z"] = p.dram("zd", [TOK, DIN], BF16)
    W["_yg"] = p.dram("ygd", [DIN, TOK], BF16)
    cdall = None
    if mode == "pre":
        W["_cd"] = p.dram("cd_out", [2080, 64], F32, "ExternalOutput")
        x_out = None
    else:
        cdall = p.dram("cdall", [4 * 2080, 64], F32, "ExternalInput")
        W["mewt"] = p.dram("mewt", [1, 4], F32, "ExternalInput")
        x_out = p.dram("x_out", [TOK, C], F32, "ExternalOutput")
    core = Core(p)
    emit_load_ident(core, cident)
    emit_l1_attn(core, W, x_in, x_out, xprev3, mode, cdall=cdall, me=me)
    return p.build()


def host_mneg():
    m = np.zeros((128, 128), np.float32)
    for t in range(128):
        m[t, t + 1:] = -1e30
    return m


def host_l1_attn_weights(inp):
    A = lambda k: np.asarray(inp[k], np.float32)
    return dict(W_inT=np.ascontiguousarray(A("L1_attn_W_in").T),
                convw=np.ascontiguousarray(A("L1_attn_conv_w").reshape(2176, 4)),
                convb=A("L1_attn_conv_b"), dt_bias=A("L1_attn_dt_bias"),
                A_log=A("L1_attn_A_log"), Dv=A("L1_attn_D"),
                norm_w=A("L1_attn_norm_w"),
                W_outT=np.ascontiguousarray(A("L1_attn_W_out").T),
                mneg=host_mneg())



_PROG_CACHE = {}


def _prog(key, builder):
    if key not in _PROG_CACHE:
        _PROG_CACHE[key] = builder()
    return _PROG_CACHE[key]


def _shards(xcur):
    return [np.ascontiguousarray(xcur[c // 4, (c % 4) * TOK:(c % 4 + 1) * TOK])
            for c in range(8)]


def _gather(res, n=8):
    out = np.zeros((2, SEQ, C), np.float32)
    for c in range(n):
        out[c // 4, (c % 4) * TOK:(c % 4 + 1) * TOK] = res[c]
    return out


def _run_ffn(xcur, nc, W, extra_fn=None):
    sh = _shards(xcur)
    in_maps = []
    for c in range(8):
        im = dict(x_in=sh[c], cident=IDENT, **W)
        if extra_fn is not None:
            im.update(extra_fn(c, xcur))
        in_maps.append(im)
    res = run_bass_kernel_spmd(nc, in_maps, core_ids=list(range(8)))
    return _gather([r["x_out"] for r in res.results])


def _xprev(c, xcur, n):
    b, j = c // 4, c % 4
    if j == 0:
        return np.zeros((n, C), np.float32)
    return np.ascontiguousarray(xcur[b, j * TOK - n:j * TOK])


_BANDS = None


def _run_attn(xcur, build, W, name, with_band=True):
    global _BANDS
    if _BANDS is None:
        _BANDS = [host_band(j * TOK) for j in range(4)]
    nc = _prog(name, build)
    sh = _shards(xcur)
    in_maps = []
    for c in range(8):
        b, j = c // 4, c % 4
        im = dict(xf=np.ascontiguousarray(xcur[b]), x_in=sh[c], cident=IDENT, **W)
        if with_band:
            im["band"] = _BANDS[j]
            im["qoffv"] = np.array([[j * TOK]], np.float32)
        in_maps.append(im)
    res = run_bass_kernel_spmd(nc, in_maps, core_ids=list(range(8)))
    return _gather([r["x_out"] for r in res.results])


def kernel(**inputs):
    inputs = {k: np.asarray(v) for k, v in inputs.items()}
    xcur = np.ascontiguousarray(inputs["L0_attn_x"].astype(np.float32))

    w_l0f = host_l0_ffn_weights(inputs)
    w_l0a = host_l0_attn_weights(inputs)
    w_l1f = host_l1_ffn_weights(inputs)
    w_l1a = host_l1_attn_weights(inputs)
    w_l2f = host_l2_ffn_weights(inputs)
    w_l2a = host_l2_attn_weights(inputs)
    w_l3f = host_l3_ffn_weights(inputs)
    w_l3a = host_l3_attn_weights(inputs)

    def l0f(x):
        nc = _prog("l0f", build_l0_ffn_program)
        return _run_ffn(x, nc, w_l0f)

    def l1f(x):
        nc = _prog("l1f", build_l1_ffn_program)
        return _run_ffn(x, nc, w_l1f,
                        extra_fn=lambda c, xc: dict(xprev=_xprev(c, xc, 1)))

    def l2f(x):
        nc = _prog("l2f", build_l2_ffn_program)
        return _run_ffn(x, nc, w_l2f)

    def l3f(x):
        nc = _prog("l3f", build_l3_ffn_program)
        return _run_ffn(x, nc, w_l3f)

    def l0a(x):
        return _run_attn(x, build_l0_attn_program, w_l0a, "l0a", with_band=False)

    def l2a(x):
        return _run_attn(x, build_l2_attn_program, w_l2a, "l2a")

    def l3a(x):
        return _run_attn(x, build_l3_attn_program, w_l3a, "l3a")

    def l1a(x):
        nc = _prog("l1a_pre", lambda: build_l1_attn_program("pre"))
        sh = _shards(x)
        in_maps = [dict(x_in=sh[c], xprev3=_xprev(c, x, 3), cident=IDENT, **w_l1a)
                   for c in range(8)]
        res = run_bass_kernel_spmd(nc, in_maps, core_ids=list(range(8)))
        cds = [r["cd_out"] for r in res.results]
        cdall = [np.ascontiguousarray(np.concatenate(cds[b * 4:(b + 1) * 4], axis=0))
                 for b in range(2)]
        nc2 = _prog("l1a_full", lambda: build_l1_attn_program("full"))
        in_maps = []
        for c in range(8):
            b, j = c // 4, c % 4
            mewt = np.zeros((1, 4), np.float32)
            mewt[0, :j] = 1.0
            in_maps.append(dict(x_in=sh[c], xprev3=_xprev(c, x, 3), cident=IDENT,
                                cdall=cdall[b], mewt=mewt, **w_l1a))
        r2 = run_bass_kernel_spmd(nc2, in_maps, core_ids=list(range(8)))
        return _gather([r["x_out"] for r in r2.results])

    import sys as _sys
    steps = [
        ("l0f", l0f), ("l0a", l0a), ("l0f2", l0f),
        ("l1f", l1f), ("l1a", l1a), ("l1f2", l1f),
        ("l2f", l2f), ("l2a", l2a), ("l2f2", l2f),
        ("l3f", l3f), ("l3a", l3a), ("l3f2", l3f),
    ]
    for nm, st in steps:
        print(f"[kernel] step {nm} ...", file=_sys.stderr, flush=True)
        xcur = st(xcur)
        print(f"[kernel] step {nm} done (nonfinite={int((~np.isfinite(xcur)).sum())})",
              file=_sys.stderr, flush=True)
    return xcur.astype(np.float32)
